# version 05
# baseline (speedup 1.0000x reference)
import numpy as np
import ml_dtypes
from contextlib import ExitStack

import concourse.bass as bass
import concourse.mybir as mybir
from concourse.bass_utils import run_bass_kernel_spmd

F32 = mybir.dt.float32
BF16 = mybir.dt.bfloat16
AF = mybir.ActivationFunctionType
ALU = mybir.AluOpType
AX = mybir.AxisListType

BF = ml_dtypes.bfloat16

B_PER_CORE = 2
T = 2048
D = 1024
NT = T // 128
NH, HS = 16, 64
NB = 127
EPS = 1e-5

ENGINES = ("sync", "gpsimd", "tensor", "vector", "scalar")

DBG = False
DBG2 = False


def pseudo_dump(p, dt_, sb_t):
    key = "dbg_dummy_" + ("bf" if sb_t.dtype == BF16 else "f32")
    if key not in dt_:
        dt_[key] = p.nc.dram_tensor(key, [128, 2048], sb_t.dtype,
                                    kind="Internal")
    st = p.sem("dbgst")
    p.need("sync", p.tok(p.sem("ve")), p.tok(p.sem("se")), p.tok(p.sem("pe")))
    c = min(sb_t.shape[1], 2048)
    p.op("sync", "dma_start", dt_[key][0:128, 0:c], sb_t[0:128, 0:c],
         inc=(st, 16))
    tok = p.tok(st)
    for e in ENGINES:
        p.need(e, tok)


def dbg_dump_nb(p, dt_, name, sb_t, cols=None, parts=128, b=0):
    if not DBG:
        return
    key = f"dbg_{name}_{b}"
    if key not in dt_:
        sh = [parts, cols if cols else sb_t.shape[1]]
        dt_[key] = p.nc.dram_tensor(key, sh, sb_t.dtype if hasattr(sb_t, "dtype")
                                    else mybir.dt.float32, kind="ExternalOutput")
    st = p.sem("dbgst")
    c = cols if cols else sb_t.shape[1]
    p.op("sync", "dma_start", dt_[key][0:parts, 0:c], sb_t[0:parts, 0:c],
         inc=(st, 16))


def dbg_dump(p, dt_, name, sb_t, cols=None, parts=128, b=0):
    if not DBG:
        return
    key = f"dbg_{name}_{b}"
    if key not in dt_:
        sh = [parts, cols if cols else sb_t.shape[1]]
        dt_[key] = p.nc.dram_tensor(key, sh, sb_t.dtype if hasattr(sb_t, "dtype")
                                    else mybir.dt.float32, kind="ExternalOutput")
    st = p.sem("dbgst")
    p.need("sync", p.tok(p.sem("ve")), p.tok(p.sem("se")), p.tok(p.sem("pe")))
    c = cols if cols else sb_t.shape[1]
    p.op("sync", "dma_start", dt_[key][0:parts, 0:c], sb_t[0:parts, 0:c],
         inc=(st, 16))
    tok = p.tok(st)
    for e in ENGINES:
        p.need(e, tok)


class Sem:
    __slots__ = ("h", "n")

    def __init__(self, h):
        self.h = h
        self.n = 0


class P:
    def __init__(self, nc, ctx):
        self.nc = nc
        self.ctx = ctx
        self.q = {e: [] for e in ENGINES}
        self.waited = {e: {} for e in ENGINES}
        self.sems = {}

    def sem(self, name):
        if name not in self.sems:
            self.sems[name] = Sem(self.ctx.enter_context(self.nc.semaphore(name)))
        return self.sems[name]

    _uid = 0

    def _nm(self, name):
        P._uid += 1
        return f"{name}__{P._uid}"

    def sb(self, name, shape, dtype, ctx=None, side=None):
        kw = {"side": side} if side else {}
        return (ctx or self.ctx).enter_context(
            self.nc.sbuf_tensor(self._nm(name), shape, dtype, **kw))

    def ps(self, name, shape, dtype, ctx=None):
        return (ctx or self.ctx).enter_context(
            self.nc.psum_tensor(self._nm(name), shape, dtype))

    def op(self, eng, meth, *a, inc=None, **k):
        toks = []
        caps = []
        if inc:
            incs = inc if isinstance(inc, list) else [inc]
            for s, n in incs:
                s.n += n
                caps.append((s.h, n))
                toks.append((s, s.n))

        def f(e, meth=meth, a=a, k=k, caps=caps):
            ins = getattr(e, meth)(*a, **k)
            for h, n in caps:
                ins = ins.then_inc(h, n)

        self.q[eng].append(f)
        return toks

    def need(self, eng, *tokens):
        for tok in tokens:
            if tok is None:
                continue
            s, v = tok
            if v > 0 and self.waited[eng].get(id(s), 0) < v:
                self.waited[eng][id(s)] = v
                self.q[eng].append(lambda e, s=s, v=v: e.wait_ge(s.h, v))

    def tok(self, sem):
        return (sem, sem.n)

    ENG_SEM = {"vector": "ve", "scalar": "se", "tensor": "pe",
               "gpsimd": "ge", "sync": "sy"}

    def opf(self, eng, meth, *a, inc=None, **k):
        s = self.sem(P.ENG_SEM[eng])
        incs = [(s, 1)]
        if inc:
            incs += inc if isinstance(inc, list) else [inc]
        toks = self.op(eng, meth, *a, inc=incs, **k)
        self.need(eng, (s, s.n))
        return toks

    def barrier(self, dma=True):
        bar = self.sem("bar")
        if dma:
            for sn, s in list(self.sems.items()):
                if sn.startswith(("ld", "st", "c_ld")) and s.n > 0:
                    t = self.tok(s)
                    for e in ENGINES:
                        self.need(e, t)
        for e in ENGINES:
            self.op(e, "drain", inc=(bar, 1))
        tgt = bar.n
        for e in ENGINES:
            self.need(e, (bar, tgt))

    def finish(self):
        qq = self.q
        with self.nc.Block() as block:
            @block.sync
            def _(e):
                for f in qq["sync"]:
                    f(e)

            @block.gpsimd
            def _(e):
                for f in qq["gpsimd"]:
                    f(e)

            @block.tensor
            def _(e):
                for f in qq["tensor"]:
                    f(e)

            @block.vector
            def _(e):
                for f in qq["vector"]:
                    f(e)

            @block.scalar
            def _(e):
                for f in qq["scalar"]:
                    f(e)



def host_consts():
    c = {}
    c["ident_bf"] = np.eye(128, dtype=BF)
    r = np.arange(128)
    c["trilI_bf"] = (r[None, :] <= r[:, None]).astype(BF)
    jj = np.arange(NB)
    c["compm0_bf"] = (jj[None, :] <= r[:, None]).astype(BF)
    s = np.arange(128)
    c["cumI_f"] = (s[:, None] <= s[None, :]).astype(np.float32)
    c["sufX_f"] = (s[:, None] > s[None, :]).astype(np.float32)
    c["cumI_r"] = (s[:, None] >= s[None, :]).astype(np.float32)
    c["sufX_r"] = (s[:, None] < s[None, :]).astype(np.float32)
    c["umaskI_f"] = (s[:, None] <= s[None, :]).astype(np.float32)
    c["umaskI_r"] = (s[:, None] >= s[None, :]).astype(np.float32)
    c["ident_f32"] = np.eye(128, dtype=np.float32)
    return c


def prep_weights(inputs):
    w = {}
    w["Wqkv"] = inputs["L0_attn_c_attn_w"]
    w["battn"] = inputs["L0_attn_c_attn_b"].reshape(1, -1)
    w["cpw"] = inputs["L0_attn_c_proj_w"]
    w["cpb"] = inputs["L0_attn_c_proj_b"].reshape(1, -1)
    w["convT"] = np.ascontiguousarray(inputs["L0_attn_conv_w"].transpose(2, 1, 0))
    w["convb"] = inputs["L0_attn_conv_b"]
    w["comp_ln_w"] = inputs["L0_attn_comp_ln_w"]
    w["comp_ln_b"] = inputs["L0_attn_comp_ln_b"]
    w["lnl_w"] = inputs["L0_attn_ln_local_w"]
    w["lnl_b"] = inputs["L0_attn_ln_local_b"]
    w["gw1"] = inputs["L0_attn_gate_w1"]
    w["gb1"] = inputs["L0_attn_gate_b1"].reshape(1, -1)
    w["gate_ln_w"] = inputs["L0_attn_gate_ln_w"].reshape(1, -1)
    w["gate_ln_b"] = inputs["L0_attn_gate_ln_b"].reshape(1, -1)
    w["gw2"] = inputs["L0_attn_gate_w2"]
    w["gb2"] = inputs["L0_attn_gate_b2"].reshape(1, -1)
    w["G1R"] = np.ascontiguousarray(inputs["L0_ffn_G1_R"][0].transpose(1, 2, 0))
    w["G1I"] = np.ascontiguousarray(inputs["L0_ffn_G1_I"][0].transpose(1, 2, 0))
    w["G2R"] = np.ascontiguousarray(inputs["L0_ffn_G2_R"][:, :, 0, :].transpose(0, 2, 1))
    w["G2I"] = np.ascontiguousarray(inputs["L0_ffn_G2_I"][:, :, 0, :].transpose(0, 2, 1))
    w["tt_bias"] = np.concatenate(
        [inputs["L0_ffn_bias_real"], inputs["L0_ffn_bias_imag"]]).reshape(1, -1)
    w["WinT"] = np.ascontiguousarray(inputs["L1_attn_W_in"].T)
    w["WoutT"] = np.ascontiguousarray(inputs["L1_attn_W_out"].T)
    for d in ("f", "r"):
        w["conv1_" + d] = inputs[f"L1_attn_conv_w_{d}"]
        w["conv1b_" + d] = np.ascontiguousarray(
            inputs[f"L1_attn_conv_b_{d}"].reshape(18, 128).T)
        w["dtb_" + d] = inputs[f"L1_attn_dt_bias_{d}"].reshape(1, -1)
        w["Alog_" + d] = inputs[f"L1_attn_A_log_{d}"].reshape(1, -1)
        w["Dsk_" + d] = inputs[f"L1_attn_D_{d}"].reshape(1, -1)
        w["nw_" + d] = inputs[f"L1_attn_norm_w_{d}"].reshape(1, -1)
    w["rsc"] = np.ascontiguousarray(
        inputs["L1_ffn_router_scale"].reshape(8, 128).T)
    w["rw"] = inputs["L1_ffn_router_w"]
    gw = inputs["L1_ffn_gating_w"]
    g0 = gw[:, 0].transpose(0, 2, 1).reshape(8, 1024, 16, 128).transpose(0, 2, 1, 3)
    g1 = gw[:, 1].transpose(0, 2, 1).reshape(8, 1024, 16, 128).transpose(0, 2, 1, 3)
    w["WgT"] = np.ascontiguousarray(np.concatenate([g0, g1], axis=3))
    w["lin"] = inputs["L1_ffn_linear_w"]
    w["pes"] = inputs["L1_ffn_per_expert_scale"].reshape(1, -1)
    return w


W_NAMES = ["Wqkv", "battn", "cpw", "cpb", "convT", "convb", "comp_ln_w",
           "comp_ln_b", "lnl_w", "lnl_b", "gw1", "gb1", "gate_ln_w",
           "gate_ln_b", "gw2", "gb2", "G1R", "G1I", "G2R", "G2I", "tt_bias",
           "WinT", "WoutT", "conv1_f", "conv1b_f", "dtb_f", "Alog_f", "Dsk_f",
           "nw_f", "conv1_r", "conv1b_r", "dtb_r", "Alog_r", "Dsk_r", "nw_r",
           "rsc", "rw", "WgT", "lin", "pes"]

PHASE_WEIGHTS = {
    "A": ["Wqkv", "battn", "cpw", "cpb", "convT", "convb", "comp_ln_w",
          "comp_ln_b", "lnl_w", "lnl_b", "gw1", "gb1", "gate_ln_w",
          "gate_ln_b", "gw2", "gb2", "G1R", "G1I", "G2R", "G2I", "tt_bias"],
    "C": ["WinT", "WoutT", "conv1_f", "conv1b_f", "dtb_f", "Alog_f", "Dsk_f",
          "nw_f", "conv1_r", "conv1b_r", "dtb_r", "Alog_r", "Dsk_r", "nw_r",
          "rsc", "rw", "pes"],
    "D": ["WgT", "lin"],
}


def declare_dram(nc, phases):
    have = set(phases)
    d = {}

    def scratch(name, shape, dtype, writer, readers):
        if writer in have and (set(readers) & have):
            kind = "ExternalOutput" if DBG else "Internal"
        elif writer in have:
            kind = "ExternalOutput"
        elif set(readers) & have:
            kind = "ExternalInput"
        else:
            return
        d[name] = nc.dram_tensor(name, shape, dtype, kind=kind)

    if "A" in have:
        d["x0"] = nc.dram_tensor("x0", [B_PER_CORE, T, D], F32, kind="ExternalInput")
    scratch("d_x2", [B_PER_CORE, T, D], BF16, "A", ["C"])
    scratch("d_x2T", [B_PER_CORE, D, T], BF16, "A", ["C"])
    scratch("d_z", [B_PER_CORE, T, 2048], BF16, "C", ["C"])
    scratch("d_xbc", [B_PER_CORE, 18, 128, T], BF16, "C", ["C"])
    scratch("d_attn1", [B_PER_CORE, T, D], F32, "C", ["C"])
    scratch("d_acrow", [4, 4096], F32, "C", ["C"])
    scratch("d_gsc", [B_PER_CORE, 2048], F32, "A", ["A"])
    scratch("d_x3", [B_PER_CORE, T, D], BF16, "C", ["D"])
    scratch("d_x3T", [B_PER_CORE, D, T], BF16, "C", ["D"])
    scratch("d_comb", [B_PER_CORE, T, 8], F32, "C", ["D"])
    if "D" in have:
        d["out"] = nc.dram_tensor("out", [B_PER_CORE, T, D], F32,
                                  kind="ExternalOutput")
    return d


def cslice(t, lo, hi):
    return t[:, lo:hi]


def blk3(t, nblk, blkw, sub_lo, sub_hi):
    return t[:, 0:nblk * blkw].rearrange("p (n w) -> p n w", n=nblk)[
        :, :, sub_lo:sub_hi]




def tiled_w_load(p, eng, dst_t, dram_ap2d, dj, sem):
    f = dram_ap2d.shape[1]
    src = dram_ap2d.rearrange("(dj p) f -> p dj f", p=128)
    dst = dst_t[:, 0:dj * f].rearrange("p (dj f) -> p dj f", dj=dj)
    return p.op(eng, "dma_start", dst, src, inc=(sem, 16))


def sb_ap(t, col_off, dims, part0=0, nparts=128):
    F = t.shape[1]
    return bass.AP(t, part0 * F + col_off, [[F, nparts]] + dims)

def bcast_load(p, dst_ap, dram_handle, count, sem, offset=0):
    nparts = dst_ap.shape[0]
    src = bass.AP(dram_handle, offset, [[0, nparts], [1, count]])
    return p.op("sync", "dma_start", dst_ap, src, inc=(sem, 16))


class Consts:
    pass


def emit_consts(p, dt_, ctx):
    C = Consts()
    ld = p.sem("c_ld")
    ve = p.sem("ve")
    C.ident = p.sb("ident", [128, 128], BF16, ctx)
    C.trilI = p.sb("trilI", [128, 128], BF16, ctx)
    C.compm0 = p.sb("compm0", [128, NB], BF16, ctx)
    p.op("sync", "dma_start", C.ident[:, :], dt_["c_ident_bf"][:, :], inc=(ld, 16))
    p.op("sync", "dma_start", C.trilI[:, :], dt_["c_trilI_bf"][:, :], inc=(ld, 16))
    p.op("sync", "dma_start", C.compm0[:, :], dt_["c_compm0_bf"][:, :], inc=(ld, 16))
    C.cum = {}
    for n in ("cumI_f", "sufX_f", "cumI_r", "sufX_r"):
        if ("c_" + n) in dt_:
            C.cum[n] = p.sb(n, [128, 128], F32, ctx)
            p.op("sync", "dma_start", C.cum[n][:, :], dt_["c_" + n][:, :],
                 inc=(ld, 16))
    C.umask = {}
    for n in ("umaskI_f", "umaskI_r"):
        if ("c_" + n) in dt_:
            C.umask[n] = p.sb(n, [128, 128], F32, ctx)
            p.op("sync", "dma_start", C.umask[n][:, :], dt_["c_" + n][:, :],
                 inc=(ld, 16))
    C.identf = None
    if "c_ident_f32" in dt_:
        C.identf = p.sb("identf", [128, 128], F32, ctx)
        p.op("sync", "dma_start", C.identf[:, :], dt_["c_ident_f32"][:, :],
             inc=(ld, 16))
    C.ones_bf = p.sb("ones_bf", [1, 512], BF16, ctx)
    C.onesf_row = p.sb("onesf_row", [1, 128], F32, ctx)
    C.onesf_col = p.sb("onesf_col", [128, 1], F32, ctx)
    C.eps = p.sb("c_eps", [128, 1], F32, ctx)
    C.eps6 = p.sb("c_eps6", [128, 1], F32, ctx)
    C.one_col = p.sb("c_onecol", [128, 1], F32, ctx)
    p.op("vector", "memset", C.ones_bf[:, :], 1.0)
    p.op("vector", "memset", C.onesf_row[:, :], 1.0)
    p.op("vector", "memset", C.onesf_col[:, :], 1.0)
    p.op("vector", "memset", C.eps[:, :], EPS)
    p.op("vector", "memset", C.eps6[:, :], 1e-6)
    p.op("vector", "memset", C.one_col[:, :], 1.0, inc=(ve, 1))
    C.ld_tok = p.tok(ld)
    C.ve_tok = p.tok(ve)
    for e in ENGINES:
        p.need(e, C.ld_tok, C.ve_tok)
    return C


def ln_tile(p, out_ap, in_ap, sq, stats, eps_col, n, scale_rep=None,
            bias_rep=None, out2=None):
    if not isinstance(eps_col, bass.AP):
        eps_col = eps_col[:, 0:1]
    ve = p.sem("ve")
    se = p.sem("se")
    p.opf("vector", "tensor_reduce", stats[:, 0:1], in_ap, AX.X, ALU.add)
    p.opf("vector", "tensor_scalar", stats[:, 0:1], stats[:, 0:1], 1.0 / n,
          None, ALU.mult)
    p.need("scalar", p.tok(ve))
    p.opf("scalar", "activation", sq, in_ap, AF.Square)
    p.need("vector", p.tok(se))
    p.opf("vector", "tensor_reduce", stats[:, 1:2], sq, AX.X, ALU.add)
    p.opf("vector", "tensor_scalar", stats[:, 1:2], stats[:, 1:2], 1.0 / n,
          None, ALU.mult)
    p.opf("vector", "tensor_tensor", stats[:, 2:3], stats[:, 0:1],
          stats[:, 0:1], ALU.mult)
    p.opf("vector", "tensor_sub", stats[:, 1:2], stats[:, 1:2], stats[:, 2:3])
    p.need("scalar", p.tok(ve))
    p.opf("scalar", "activation", stats[:, 2:3], stats[:, 1:2], AF.Ln,
          bias=eps_col)
    p.opf("scalar", "activation", stats[:, 2:3], stats[:, 2:3], AF.Exp,
          scale=-0.5)
    p.need("vector", p.tok(se))
    tok = p.opf("vector", "tensor_scalar", out_ap, in_ap, stats[:, 0:1],
                stats[:, 2:3], ALU.subtract, ALU.mult)
    if scale_rep is not None:
        p.opf("vector", "tensor_mul", out_ap, out_ap, scale_rep)
        tok = p.opf("vector", "tensor_add", out_ap, out_ap, bias_rep)
    if out2 is not None:
        tok = p.opf("vector", "tensor_copy", out2, out_ap)
    return tok[0]



def emit_phaseA(p, dt_, C):
    ldg = p.sem("ldg")
    ld = p.sem("ld")
    with ExitStack() as actx:
        battn = p.sb("battn", [1, 3072], BF16, actx)
        p.op("gpsimd", "dma_start", battn[:, :], dt_["w_battn"][:, :],
             inc=(ldg, 16))
        convW = p.sb("convW", [128, 32 * 128], BF16, actx)
        convb_col = p.sb("convb_col", [128, 1], F32, actx)
        emit_build_convw(p, dt_, convW, convb_col)
        clnw = p.sb("clnw", [128, 64], F32, actx)
        clnb = p.sb("clnb", [128, 64], F32, actx)
        bcast_load(p, clnw[:, :], dt_["w_comp_ln_w"], 64, ld)
        bcast_load(p, clnb[:, :], dt_["w_comp_ln_b"], 64, ld)
        for b in range(B_PER_CORE):
            emit_phaseA_batch(p, dt_, C, b, battn, convW, convb_col, clnw, clnb)
    p.barrier()


def emit_build_convw(p, dt_, convW, convb_col):
    ld = p.sem("ld")
    ve = p.sem("ve")
    ldg = p.sem("ldg")
    t = p.op("vector", "memset", convW[:, :], 0.0, inc=(ve, 1))
    p.need("gpsimd", t[0])
    cw = dt_["w_convT"]
    for k in range(32):
        src = bass.AP(cw, k * 64 * 64, [[64, 64], [1, 64]])
        p.op("gpsimd", "dma_start", convW[0:64, k * 128:k * 128 + 64], src,
             inc=(ldg, 16))
        p.op("gpsimd", "dma_start", convW[64:128, k * 128 + 64:k * 128 + 128],
             src, inc=(ldg, 16))
    p.op("sync", "dma_start", convb_col[:, 0:1],
         bass.AP(dt_["w_convb"], 0, [[0, 2], [1, 64]]), inc=(ld, 16))


def emit_phaseA_batch(p, dt_, C, b, battn, convW, convb_col, clnw, clnb):
    pe = p.sem("pe")
    ve = p.sem("ve")
    se = p.sem("se")
    ld = p.sem("ld")
    ldg = p.sem("ldg")
    st = p.sem("st")
    x0 = dt_["x0"]

    with ExitStack() as sc_big:
        local = None
        comp = None
        with ExitStack() as sc_attn:
            qkT = p.sb("qkT", [128, 16 * T], BF16, sc_attn)
            Vtm = p.sb("Vtm", [128, 16 * 1024], BF16, sc_attn)
            kcomp = p.sb("kcomp", [128, 16 * 64], BF16, sc_attn)
            vcomp = p.sb("vcomp", [128, 16 * 64], BF16, sc_attn)
            kcompT = p.sb("kcompT", [128, 8 * NB], BF16, sc_attn)

            with ExitStack() as sc_b:
                emit_qkv(p, dt_, C, b, battn, qkT, Vtm, kcomp, vcomp, kcompT,
                         convW, convb_col, clnw, clnb, sc_b)
            p.barrier()

            local = p.sb("local", [128, 16 * 1024], BF16, sc_big, side="right")
            comp = p.sb("comp", [128, 16 * 1024], BF16, sc_big, side="right")
            with ExitStack() as sc_at:
                emit_attention(p, C, b, qkT, Vtm, kcomp, vcomp, kcompT,
                               local, comp, sc_at)
            p.barrier()
        dbg_dump(p, dt_, "local", local, b=b)
        dbg_dump(p, dt_, "comp", comp, b=b)
        with ExitStack() as s2:
            emit_phaseA_tail(p, dt_, C, b, local, comp, s2)
        p.barrier()


def emit_qkv(p, dt_, C, b, battn, qkT, Vtm, kcomp, vcomp, kcompT, convW,
             convb_col, clnw, clnb, sc_b):
    pe = p.sem("pe")
    ve = p.sem("ve")
    se = p.sem("se")
    ld = p.sem("ld")
    ldg = p.sem("ldg")
    x0 = dt_["x0"]

    x0T = p.sb("x0T", [128, 8 * T], BF16, sc_b)
    vT = p.sb("vT", [128, 8 * T], BF16, sc_b)
    xrow = [p.sb(f"xrow{i}", [128, 1024], BF16, sc_b) for i in range(2)]
    wbuf = [p.sb(f"wbuf{i}", [128, 8 * 128], BF16, sc_b) for i in range(2)]
    psT = p.ps("psT", [128, 1024], BF16, sc_b)
    psM = [p.ps(f"psM{i}", [128, 512], F32, sc_b) for i in range(2)]

    xrow_free = [None, None]
    psT_free = None
    for ti in range(NT):
        buf = xrow[ti % 2]
        p.need("gpsimd", xrow_free[ti % 2])
        lt = p.op("gpsimd", "dma_start", buf[:, :],
                  x0[b, ti * 128:(ti + 1) * 128, :],
                  inc=(p.sem(f"ldg_xr{ti % 2}"), 16))
        p.need("tensor", lt[0], psT_free)
        for dj in range(8):
            p.op("tensor", "transpose", psT[:, dj * 128:(dj + 1) * 128],
                 buf[:, dj * 128:(dj + 1) * 128], C.ident[:, :],
                 inc=(pe, 1) if dj == 7 else None)
        p.need("vector", p.tok(pe))
        et = p.op("vector", "tensor_copy",
                  blk3(x0T, 8, T, ti * 128, (ti + 1) * 128),
                  blk3(psT, 8, 128, 0, 128), inc=(ve, 1))
        xrow_free[ti % 2] = et[0]
        psT_free = et[0]

    wbuf_free = [None, None]
    psM_free = [None, None]
    for fi in range(24):
        wb = wbuf[fi % 2]
        p.need("gpsimd", wbuf_free[fi % 2])
        wt = tiled_w_load(p, "gpsimd", wb,
                          dt_["w_Wqkv"][:, fi * 128:(fi + 1) * 128], 8,
                          p.sem(f"ldg_qw{fi % 2}"))
        dstT = qkT if fi < 16 else vT
        dbase = fi * T if fi < 16 else (fi - 16) * T
        for tc in range(4):
            pm = psM[tc % 2]
            p.need("tensor", psM_free[tc % 2], wt[0])
            for dj in range(8):
                p.op("tensor", "matmul", pm[:, :],
                     wb[:, dj * 128:(dj + 1) * 128],
                     x0T[:, dj * T + tc * 512:dj * T + (tc + 1) * 512],
                     start=(dj == 0), stop=False)
            p.op("tensor", "matmul", pm[:, :],
                 battn[0:1, fi * 128:(fi + 1) * 128], C.ones_bf[0:1, :],
                 start=False, stop=True, inc=(pe, 1))
            p.need("vector", p.tok(pe))
            et = p.op("vector", "tensor_copy",
                      dstT[:, dbase + tc * 512:dbase + (tc + 1) * 512],
                      pm[:, :], inc=(ve, 1))
            psM_free[tc % 2] = et[0]
        wbuf_free[fi % 2] = p.tok(pe)

    psT_free = p.tok(ve)
    for ti in range(NT):
        p.need("tensor", psT_free, p.tok(ve))
        for dj in range(8):
            p.op("tensor", "transpose", psT[:, dj * 128:(dj + 1) * 128],
                 vT[:, dj * T + ti * 128:dj * T + (ti + 1) * 128],
                 C.ident[:, :], inc=(pe, 1) if dj == 7 else None)
        p.need("vector", p.tok(pe))
        et = p.op("vector", "tensor_copy", Vtm[:, ti * 1024:(ti + 1) * 1024],
                  psT[:, :], inc=(ve, 1))
        psT_free = et[0]

    cstat = p.sb("cstat", [128, 4], F32, sc_b)
    csq = p.sb("csq", [128, 128], F32, sc_b)
    csqb = p.sb("csqb", [128, 128], BF16, sc_b)
    ctmp = p.sb("ctmp", [128, 128], F32, sc_b)
    psC = p.ps("psC", [128, 128], F32, sc_b)
    psCT = p.ps("psCT", [128, 128], BF16, sc_b)
    psC_free = None
    psCT_free = None
    gelu_tok = None
    for (srcT, dst) in ((qkT, kcomp), (vT, vcomp)):
        koff = 8 * T if srcT is qkT else 0
        for pt in range(8):
            p.need("tensor", psC_free, p.tok(ve), p.tok(ldg))
            base = koff + pt * T
            for k in range(32):
                rhs = sb_ap(srcT, base + k, [[16, NB]])
                p.op("tensor", "matmul", psC[:, 0:NB],
                     convW[:, k * 128:(k + 1) * 128], rhs,
                     start=(k == 0), stop=(k == 31),
                     inc=(pe, 1) if k == 31 else None)
            p.need("vector", p.tok(pe))
            t1 = p.op("vector", "tensor_scalar", csqb[:, 0:NB], psC[:, 0:NB],
                      convb_col[:, 0:1], None, ALU.add, inc=(ve, 1))
            psC_free = t1[0]
            if pt == 0 and srcT is qkT:
                dbg_dump(p, dt_, "conv0", csqb, b=b)
            p.need("tensor", t1[0], psCT_free)
            p.op("tensor", "transpose", psCT[:, :], csqb[:, 0:128],
                 C.ident[:, :], inc=(pe, 1))
            p.need("vector", p.tok(pe))
            p.need("vector", gelu_tok)
            t2 = p.op("vector", "tensor_copy", ctmp[0:NB, 0:128],
                      psCT[0:NB, :], inc=(ve, 1))
            psCT_free = t2[0]
            if pt == 0 and srcT is qkT:
                dbg_dump(p, dt_, "ctmp0", ctmp, b=b)
            gelu_tok = _ln_seg64(p, dst[0:NB, pt * 128:(pt + 1) * 128], ctmp,
                                 cstat, csq, clnw, clnb, C.eps)
            if pt == 0 and srcT is qkT:
                dbg_dump(p, dt_, "cstat0", cstat, b=b)
                dbg_dump(p, dt_, "lnout0", ctmp, b=b)
    dbg_dump(p, dt_, "qkT", qkT, b=b)
    dbg_dump(p, dt_, "Vtm", Vtm, b=b)
    dbg_dump(p, dt_, "kcomp", kcomp, b=b)
    dbg_dump(p, dt_, "vcomp", vcomp, b=b)
    for pt in range(8):
        p.need("tensor", p.tok(se), psCT_free)
        p.op("tensor", "transpose", psCT[:, :],
             kcomp[0:128, pt * 128:(pt + 1) * 128], C.ident[:, :], inc=(pe, 1))
        p.need("vector", p.tok(pe))
        t2 = p.op("vector", "tensor_copy", kcompT[:, pt * NB:(pt + 1) * NB],
                  psCT[:, 0:NB], inc=(ve, 1))
        psCT_free = t2[0]


def _ln_seg64(p, out_ap, ctmp, cstat, csq, clnw, clnb, eps_col):
    ve = p.sem("ve")
    se = p.sem("se")
    src3 = ctmp[0:NB, 0:128].rearrange("p (h s) -> p h s", h=2)
    st2 = cstat[0:NB, 0:2].unsqueeze(2)
    st24 = cstat[0:NB, 2:4].unsqueeze(2)
    p.opf("vector", "tensor_reduce", st2, src3, AX.X, ALU.add)
    p.opf("vector", "tensor_scalar", cstat[0:NB, 0:2], cstat[0:NB, 0:2],
          1.0 / 64, None, ALU.mult)
    p.need("scalar", p.tok(ve))
    p.opf("scalar", "activation", csq[0:NB, 0:128], ctmp[0:NB, 0:128],
          AF.Square)
    p.need("vector", p.tok(se))
    p.opf("vector", "tensor_reduce", st24,
          csq[0:NB, 0:128].rearrange("p (h s) -> p h s", h=2), AX.X, ALU.add)
    p.opf("vector", "tensor_scalar", cstat[0:NB, 2:4], cstat[0:NB, 2:4],
          1.0 / 64, None, ALU.mult)
    p.opf("vector", "tensor_tensor", csq[0:NB, 0:2], cstat[0:NB, 0:2],
          cstat[0:NB, 0:2], ALU.mult)
    p.opf("vector", "tensor_sub", cstat[0:NB, 2:4], cstat[0:NB, 2:4],
          csq[0:NB, 0:2])
    p.need("scalar", p.tok(ve))
    p.opf("scalar", "activation", cstat[0:NB, 2:4], cstat[0:NB, 2:4], AF.Ln,
          bias=eps_col[0:NB, 0:1])
    p.opf("scalar", "activation", cstat[0:NB, 2:4], cstat[0:NB, 2:4], AF.Exp,
          scale=-0.5)
    p.need("vector", p.tok(se))
    for h2 in range(2):
        p.opf("vector", "tensor_scalar", ctmp[0:NB, h2 * 64:(h2 + 1) * 64],
              ctmp[0:NB, h2 * 64:(h2 + 1) * 64], cstat[0:NB, h2:h2 + 1],
              cstat[0:NB, 2 + h2:3 + h2], ALU.subtract, ALU.mult)
    v3 = ctmp[0:NB, 0:128].rearrange("p (h s) -> p h s", h=2)
    p.opf("vector", "tensor_mul", v3, v3,
          clnw[0:NB, :].unsqueeze(1).broadcast_to((NB, 2, 64)))
    p.opf("vector", "tensor_add", v3, v3,
          clnb[0:NB, :].unsqueeze(1).broadcast_to((NB, 2, 64)))
    p.need("scalar", p.tok(ve))
    t = p.opf("scalar", "activation", out_ap, ctmp[0:NB, 0:128], AF.Gelu,
              inc=(se, 0) if False else None)
    return t[0]


def emit_attention(p, C, b, qkT, Vtm, kcomp, vcomp, kcompT, local, comp, s1):
    pe = p.sem("pe")
    ve = p.sem("ve")
    se = p.sem("se")
    Ssb = p.sb("Ssb", [128, 2048], BF16, s1)
    PTsb = p.sb("PTsb", [128, 2048], BF16, s1)
    nmax = p.sb("nmax", [128, 4], F32, s1)
    rsum = p.sb("rsum", [128, 4], F32, s1)
    psS = [p.ps(f"psS{i}", [128, 1024], F32, s1) for i in range(2)]
    psPT = p.ps("psPT", [128, 1024], BF16, s1)
    psO = [p.ps(f"psO{i}", [128, 64], F32, s1) for i in range(2)]

    psS_free = [None, None]
    psPT_free = None
    psO_free = [None, None]

    for h in range(NH):
        qpt = h // 2
        qbase = (h % 2) * 64
        kpt = 8 + h // 2
        for qi in range(NT):
            kend = (qi + 1) * 128
            nblk = 1 if kend <= 1024 else 2
            use = [0, 1] if nblk == 2 else [qi % 2]
            stoks = []
            for i, blk in enumerate(use):
                pm = psS[blk]
                c0 = i * 1024
                c1 = min(kend, (i + 1) * 1024)
                p.need("tensor", psS_free[blk])
                nch = (c1 - c0 + 511) // 512
                for cc in range(nch):
                    w0 = c0 + cc * 512
                    w1 = min(c1, w0 + 512)
                    p.op("tensor", "matmul", pm[:, w0 - c0:w1 - c0],
                         qkT[qbase:qbase + 64,
                             qpt * T + qi * 128:qpt * T + (qi + 1) * 128],
                         qkT[qbase:qbase + 64, kpt * T + w0:kpt * T + w1],
                         start=True, stop=True,
                         inc=(pe, 1) if cc == nch - 1 else None)
                stoks.append(p.tok(pe))
            p.need("vector", *stoks)
            for i, blk in enumerate(use):
                c0 = i * 1024
                c1 = min(kend, (i + 1) * 1024)
                p.opf("vector", "tensor_reduce", nmax[:, i:i + 1],
                      psS[blk][:, 0:c1 - c0], AX.X, ALU.max, negate=True)
            if nblk == 2:
                p.opf("vector", "tensor_tensor", nmax[:, 0:1], nmax[:, 0:1],
                      nmax[:, 1:2], ALU.min)
            p.opf("vector", "tensor_scalar", nmax[:, 2:3], nmax[:, 0:1], 0.125,
                  None, ALU.mult)
            p.need("scalar", p.tok(ve))
            dstart = qi * 128
            nacc = 0
            for i, blk in enumerate(use):
                c0 = i * 1024
                c1 = min(dstart, (i + 1) * 1024)
                if c1 > c0:
                    p.op("scalar", "activation", Ssb[:, c0:c1],
                         psS[blk][:, 0:c1 - c0], AF.Exp, bias=nmax[:, 2:3],
                         scale=0.125, accum_out=rsum[:, i:i + 1], inc=(se, 1))
                    nacc += 1
            blkd = dstart // 1024 if nblk == 2 else 0
            pmD = psS[use[blkd]]
            p.op("scalar", "activation", Ssb[:, dstart:kend],
                 pmD[:, dstart - blkd * 1024:kend - blkd * 1024], AF.Exp,
                 bias=nmax[:, 2:3], scale=0.125, inc=(se, 1))
            exp_done = p.tok(se)
            for blk in use:
                psS_free[blk] = exp_done
            p.need("vector", exp_done)
            p.opf("vector", "tensor_mul", Ssb[:, dstart:kend],
                  Ssb[:, dstart:kend], C.trilI[:, :])
            p.opf("vector", "tensor_reduce", rsum[:, 2:3],
                  Ssb[:, dstart:kend], AX.X, ALU.add)
            for i in range(nacc):
                p.opf("vector", "tensor_add", rsum[:, 2:3], rsum[:, 2:3],
                      rsum[:, i:i + 1])
            p.opf("vector", "reciprocal", rsum[:, 3:4], rsum[:, 2:3])
            sm_done = p.tok(p.sem("ve"))
            ntile = qi + 1
            evtoks = []
            for r0 in range(0, ntile, 8):
                r1 = min(ntile, r0 + 8)
                p.need("tensor", sm_done, psPT_free)
                for kt in range(r0, r1):
                    p.op("tensor", "transpose",
                         psPT[:, (kt - r0) * 128:(kt - r0 + 1) * 128],
                         Ssb[:, kt * 128:(kt + 1) * 128], C.ident[:, :],
                         inc=(pe, 1) if kt == r1 - 1 else None)
                p.need("vector", p.tok(pe))
                et = p.op("vector", "tensor_copy", PTsb[:, r0 * 128:r1 * 128],
                          psPT[:, 0:(r1 - r0) * 128], inc=(ve, 1))
                psPT_free = et[0]
                evtoks.append(et[0])
            p.need("tensor", psO_free[qi % 2], *evtoks)
            for kt in range(ntile):
                p.op("tensor", "matmul", psO[qi % 2][:, :],
                     PTsb[:, kt * 128:(kt + 1) * 128],
                     Vtm[:, kt * 1024 + h * 64:kt * 1024 + (h + 1) * 64],
                     start=(kt == 0), stop=(kt == ntile - 1),
                     inc=(pe, 1) if kt == ntile - 1 else None)
            p.need("scalar", p.tok(pe), sm_done)
            ot = p.op("scalar", "activation",
                      local[:, qi * 1024 + h * 64:qi * 1024 + (h + 1) * 64],
                      psO[qi % 2][:, :], AF.Copy, scale=rsum[:, 3:4],
                      inc=(se, 1))
            psO_free[qi % 2] = ot[0]

        for qi in range(NT):
            pm = psS[qi % 2]
            p.need("tensor", psS_free[qi % 2])
            p.op("tensor", "matmul", pm[:, 0:NB],
                 qkT[qbase:qbase + 64,
                     qpt * T + qi * 128:qpt * T + (qi + 1) * 128],
                 kcompT[qbase:qbase + 64, (h // 2) * NB:(h // 2 + 1) * NB],
                 start=True, stop=True, inc=(pe, 1))
            p.need("vector", p.tok(pe))
            p.opf("vector", "tensor_reduce", nmax[:, 0:1], pm[:, 0:NB], AX.X,
                  ALU.max, negate=True)
            p.opf("vector", "tensor_scalar", nmax[:, 2:3], nmax[:, 0:1], 0.125,
                  None, ALU.mult)
            p.need("scalar", p.tok(ve))
            if qi == 0:
                p.op("scalar", "activation", Ssb[:, 0:NB], pm[:, 0:NB], AF.Exp,
                     bias=nmax[:, 2:3], scale=0.125, inc=(se, 1))
                psS_free[qi % 2] = p.tok(se)
                p.need("vector", p.tok(se))
                p.opf("vector", "tensor_mul", Ssb[:, 0:NB], Ssb[:, 0:NB],
                      C.compm0[:, :])
                p.opf("vector", "tensor_reduce", rsum[:, 2:3], Ssb[:, 0:NB],
                      AX.X, ALU.add)
            else:
                p.op("scalar", "activation", Ssb[:, 0:NB], pm[:, 0:NB], AF.Exp,
                     bias=nmax[:, 2:3], scale=0.125, accum_out=rsum[:, 2:3],
                     inc=(se, 1))
                psS_free[qi % 2] = p.tok(se)
                p.need("vector", p.tok(se))
            p.opf("vector", "reciprocal", rsum[:, 3:4], rsum[:, 2:3])
            rc_done = p.tok(p.sem("ve"))
            p.need("tensor", p.tok(ve), psPT_free)
            p.op("tensor", "transpose", psPT[:, 0:128], Ssb[:, 0:128],
                 C.ident[:, :], inc=(pe, 1))
            p.need("vector", p.tok(pe))
            et = p.op("vector", "tensor_copy", PTsb[:, 0:128], psPT[:, 0:128],
                      inc=(ve, 1))
            psPT_free = et[0]
            p.need("tensor", et[0], psO_free[qi % 2])
            p.op("tensor", "matmul", psO[qi % 2][:, :], PTsb[0:NB, 0:128],
                 vcomp[0:NB, h * 64:(h + 1) * 64], start=True, stop=True,
                 inc=(pe, 1))
            p.need("scalar", p.tok(pe), rc_done)
            ot = p.op("scalar", "activation",
                      comp[:, qi * 1024 + h * 64:qi * 1024 + (h + 1) * 64],
                      psO[qi % 2][:, :], AF.Copy, scale=rsum[:, 3:4],
                      inc=(se, 1))
            psO_free[qi % 2] = ot[0]


def emit_phaseA_tail(p, dt_, C, b, local, comp, s2):
    pe = p.sem("pe")
    ve = p.sem("ve")
    se = p.sem("se")
    ld = p.sem("ld")
    ldg = p.sem("ldg")
    st = p.sem("st")
    x0 = dt_["x0"]

    lnlw = p.sb("lnlw", [128, 1024], F32, s2)
    lnlb = p.sb("lnlb", [128, 1024], F32, s2)
    bcast_load(p, lnlw[:, :], dt_["w_lnl_w"], 1024, ld)
    bcast_load(p, lnlb[:, :], dt_["w_lnl_b"], 1024, ld)
    cpw = p.sb("cpw", [128, 8 * 1024], BF16, s2)
    tiled_w_load(p, "gpsimd", cpw, dt_["w_cpw"][:, :], 8, ldg)
    cpb = p.sb("cpb", [1, 1024], BF16, s2)
    p.op("gpsimd", "dma_start", cpb[:, :], dt_["w_cpb"][:, :], inc=(ldg, 16))
    Mbig = p.sb("Mbig", [128, 8 * 1024], BF16, s2)
    emit_build_mbig(p, dt_, Mbig, s2)
    ttb = p.sb("ttb", [1, 1024], BF16, s2)
    p.op("gpsimd", "dma_start", ttb[:, :], dt_["w_tt_bias"][:, :], inc=(ldg, 16))

    grep = p.sb("grep", [128, 2], F32, s2)
    emit_gates(p, dt_, C, b, local, comp, lnlw, lnlb, grep, s2)

    stats = p.sb("t_stats", [128, 4], F32, s2)
    sq = p.sb("t_sq", [128, 1024], F32, s2)
    lnl = p.sb("t_lnl", [128, 1024], F32, s2)
    cg = p.sb("t_cg", [128, 1024], F32, s2)
    cmb_bf = p.sb("t_cmb", [128, 1024], BF16, s2)
    cT = p.sb("t_cT", [128, 1024], BF16, s2)
    x0row = [p.sb(f"t_x0r{i}", [128, 1024], F32, s2) for i in range(2)]
    x1t = p.sb("t_x1", [128, 1024], F32, s2)
    x1b = p.sb("t_x1b", [128, 1024], BF16, s2)
    x1T = p.sb("t_x1T", [128, 1024], BF16, s2)
    x2t = p.sb("t_x2t", [128, 1024], F32, s2)
    x2b = p.sb("t_x2b", [128, 1024], BF16, s2)
    x2Tb = p.sb("t_x2Tb", [128, 1024], BF16, s2)
    psT2 = p.ps("psT2", [128, 1024], BF16, s2)
    psP = [p.ps(f"psP{i}", [128, 512], F32, s2) for i in range(2)]

    x0r_free = [None, None]
    psT2_free = None
    psP_free = [None, None]
    x2b_free = None
    x2Tb_free = None
    for ti in range(NT):
        p.need("sync", x0r_free[ti % 2])
        lt = p.op("sync", "dma_start", x0row[ti % 2][:, :],
                  x0[b, ti * 128:(ti + 1) * 128, :],
                  inc=(p.sem(f"ld_x0r{ti % 2}"), 16))
        ln_tile(p, lnl[:, :], local[:, ti * 1024:(ti + 1) * 1024], sq[:, :],
                stats, C.eps, 1024, scale_rep=lnlw[:, :], bias_rep=lnlb[:, :])
        p.op("vector", "tensor_scalar", cg[:, :],
             comp[:, ti * 1024:(ti + 1) * 1024], grep[:, 1:2], None, ALU.mult)
        p.op("vector", "scalar_tensor_tensor", cmb_bf[:, :], lnl[:, :],
             grep[:, 0:1], cg[:, :], ALU.mult, ALU.add, inc=(ve, 1))
        p.need("tensor", p.tok(ve), psT2_free)
        for dj in range(8):
            p.op("tensor", "transpose", psT2[:, dj * 128:(dj + 1) * 128],
                 cmb_bf[:, dj * 128:(dj + 1) * 128], C.ident[:, :],
                 inc=(pe, 1) if dj == 7 else None)
        p.need("vector", p.tok(pe))
        et = p.op("vector", "tensor_copy", cT[:, :], psT2[:, :], inc=(ve, 1))
        psT2_free = et[0]
        p.need("tensor", et[0], p.tok(ldg))
        mtoks = []
        for fc in range(2):
            p.need("tensor", psP_free[fc])
            for cj in range(8):
                p.op("tensor", "matmul", psP[fc][:, :],
                     cT[:, cj * 128:(cj + 1) * 128],
                     cpw[:, cj * 1024 + fc * 512:cj * 1024 + (fc + 1) * 512],
                     start=(cj == 0), stop=False)
            p.op("tensor", "matmul", psP[fc][:, :], C.ones_bf[0:1, 0:128],
                 cpb[0:1, fc * 512:(fc + 1) * 512], start=False, stop=True,
                 inc=(pe, 1))
            mtoks.append(p.tok(pe))
        p.need("vector", *mtoks, lt[0])
        for fc in range(2):
            vt = p.op("vector", "tensor_add", x1t[:, fc * 512:(fc + 1) * 512],
                      x0row[ti % 2][:, fc * 512:(fc + 1) * 512], psP[fc][:, :],
                      inc=(ve, 1))
            psP_free[fc] = vt[0]
        x0r_free[ti % 2] = p.tok(ve)
        ln_tile(p, x1t[:, :], x1t[:, :], sq[:, :], stats, C.eps, 1024,
                out2=x1b[:, :])
        p.need("tensor", p.tok(ve), psT2_free)
        for dj in range(8):
            p.op("tensor", "transpose", psT2[:, dj * 128:(dj + 1) * 128],
                 x1b[:, dj * 128:(dj + 1) * 128], C.ident[:, :],
                 inc=(pe, 1) if dj == 7 else None)
        p.need("vector", p.tok(pe))
        et = p.op("vector", "tensor_copy", x1T[:, :], psT2[:, :], inc=(ve, 1))
        psT2_free = et[0]
        p.need("tensor", et[0])
        mtoks = []
        for fc in range(2):
            p.need("tensor", psP_free[fc])
            for cj in range(8):
                p.op("tensor", "matmul", psP[fc][:, :],
                     x1T[:, cj * 128:(cj + 1) * 128],
                     Mbig[:, cj * 1024 + fc * 512:cj * 1024 + (fc + 1) * 512],
                     start=(cj == 0), stop=False)
            p.op("tensor", "matmul", psP[fc][:, :], C.ones_bf[0:1, 0:128],
                 ttb[0:1, fc * 512:(fc + 1) * 512], start=False, stop=True,
                 inc=(pe, 1))
            mtoks.append(p.tok(pe))
        p.need("vector", *mtoks)
        for fc in range(2):
            vt = p.op("vector", "tensor_add", x2t[:, fc * 512:(fc + 1) * 512],
                      x1t[:, fc * 512:(fc + 1) * 512], psP[fc][:, :],
                      inc=(ve, 1))
            psP_free[fc] = vt[0]
        p.need("vector", x2b_free)
        ln_tile(p, x2t[:, :], x2t[:, :], sq[:, :], stats, C.eps, 1024,
                out2=x2b[:, :])
        p.need("sync", p.tok(ve))
        p.op("sync", "dma_start", dt_["d_x2"][b, ti * 128:(ti + 1) * 128, :],
             x2b[:, :], inc=(p.sem("st_x2"), 16))
        x2b_free = p.tok(p.sem("st_x2"))
        p.need("tensor", p.tok(ve), psT2_free)
        for dj in range(8):
            p.op("tensor", "transpose", psT2[:, dj * 128:(dj + 1) * 128],
                 x2b[:, dj * 128:(dj + 1) * 128], C.ident[:, :],
                 inc=(pe, 1) if dj == 7 else None)
        p.need("vector", p.tok(pe), x2Tb_free)
        et = p.op("vector", "tensor_copy", x2Tb[:, :], psT2[:, :], inc=(ve, 1))
        psT2_free = et[0]
        p.need("sync", et[0])
        dstT = bass.AP(dt_["d_x2T"], b * D * T + ti * 128,
                       [[T, 128], [128 * T, 8], [1, 128]])
        p.op("sync", "dma_start", dstT,
             x2Tb[:, :].rearrange("p (dj c) -> p dj c", dj=8),
             inc=(p.sem("st_x2T"), 16))
        x2Tb_free = p.tok(p.sem("st_x2T"))
    p.need("sync", p.tok(st), p.tok(p.sem("st_x2")), p.tok(p.sem("st_x2T")))


def emit_build_mbig(p, dt_, Mbig, s2):
    ld = p.sem("ld")
    ve = p.sem("ve")
    mctx = ExitStack()
    ta = p.sb("mb_ta", [128, 32], F32, mctx)
    tb = p.sb("mb_tb", [128, 16], F32, mctx)
    acc = {w: p.sb("mb_acc" + w, [128, 4 * 512], F32, mctx) for w in ("R", "I")}
    tmp = p.sb("mb_tmp", [128, 512], F32, mctx)
    ta_free = None
    for which in ("R", "I"):
        g1 = dt_["w_G1" + which]
        g2 = dt_["w_G2" + which]
        for c in range(4):
            for pt in range(4):
                p.need("sync", ta_free)
                srcap = bass.AP(g1, c * 32 * 32 + 8 * pt * 32,
                                [[32, 8], [0, 16], [1, 32]])
                p.op("sync", "dma_start", ta[:, :], srcap,
                     inc=(p.sem("ld_mb"), 16))
                srcbp = bass.AP(g2, c * 16 * 16,
                                [[0, 8], [16, 16], [1, 16]])
                p.op("sync", "dma_start", tb[:, :], srcbp,
                     inc=(p.sem("ld_mb"), 16))
                p.need("vector", p.tok(p.sem("ld_mb")))
                p.op("vector", "tensor_tensor",
                     tmp[:, :].rearrange("p (b e) -> p b e", e=16),
                     ta[:, :].unsqueeze(2).broadcast_to((128, 32, 16)),
                     tb[:, :].unsqueeze(1).broadcast_to((128, 32, 16)), ALU.mult)
                if c == 0:
                    t = p.op("vector", "tensor_copy",
                             acc[which][:, pt * 512:(pt + 1) * 512], tmp[:, :],
                             inc=(ve, 1))
                else:
                    t = p.op("vector", "tensor_add",
                             acc[which][:, pt * 512:(pt + 1) * 512],
                             acc[which][:, pt * 512:(pt + 1) * 512], tmp[:, :],
                             inc=(ve, 1))
                ta_free = t[0]
    for j in range(8):
        if j < 4:
            p.op("vector", "tensor_copy", Mbig[:, j * 1024:j * 1024 + 512],
                 acc["R"][:, j * 512:(j + 1) * 512])
            p.op("vector", "tensor_copy",
                 Mbig[:, j * 1024 + 512:(j + 1) * 1024],
                 acc["I"][:, j * 512:(j + 1) * 512], inc=(ve, 1))
        else:
            jj = j - 4
            p.op("vector", "tensor_scalar", Mbig[:, j * 1024:j * 1024 + 512],
                 acc["I"][:, jj * 512:(jj + 1) * 512], -1.0, None, ALU.mult)
            p.op("vector", "tensor_copy",
                 Mbig[:, j * 1024 + 512:(j + 1) * 1024],
                 acc["I" if False else "R"][:, jj * 512:(jj + 1) * 512],
                 inc=(ve, 1))
    p.barrier()
    mctx.close()


def emit_gates(p, dt_, C, b, local, comp, lnlw, lnlb, grep, s2):
    pe = p.sem("pe")
    ve = p.sem("ve")
    se = p.sem("se")
    ld = p.sem("ld")
    ldg = p.sem("ldg")
    st = p.sem("st")

    gctx = ExitStack()
    s2 = gctx
    lrow_bf = p.sb("g_lrow", [1, 2048], BF16, s2)
    feats = p.sb("g_feats", [1, 2048], F32, s2)
    gst = p.sb("g_stats", [1, 4], F32, s2)
    gsq = p.sb("g_sq", [1, 2048], F32, s2)
    featC = p.sb("g_featC", [128, 16], F32, s2)
    featCb = p.sb("g_featCb", [128, 16], BF16, s2)
    h1 = p.sb("g_h1", [1, 1024], F32, s2)
    hC = p.sb("g_hC", [128, 8], F32, s2)
    hCb = p.sb("g_hCb", [128, 8], BF16, s2)
    gw1t = p.sb("g_w1t", [128, 16 * 1024], BF16, s2)
    gw2t = p.sb("g_w2t", [128, 8 * 2], BF16, s2)
    glnw = p.sb("g_lnw", [1, 1024], F32, s2)
    glnb = p.sb("g_lnb", [1, 1024], F32, s2)
    gb1 = p.sb("g_b1", [1, 1024], BF16, s2)
    gb2 = p.sb("g_b2", [1, 2], BF16, s2)
    grow = p.sb("g_row", [1, 1024], F32, s2)
    psG = p.ps("psG", [128, 1024], F32, s2)

    tiled_w_load(p, "gpsimd", gw1t, dt_["w_gw1"][:, :], 16, ldg)
    tiled_w_load(p, "gpsimd", gw2t, dt_["w_gw2"][:, :], 8, ldg)
    p.op("sync", "dma_start", glnw[:, :], dt_["w_gate_ln_w"][:, :], inc=(ld, 16))
    p.op("sync", "dma_start", glnb[:, :], dt_["w_gate_ln_b"][:, :], inc=(ld, 16))
    p.op("gpsimd", "dma_start", gb1[:, :], dt_["w_gb1"][:, :], inc=(ldg, 16))
    p.op("gpsimd", "dma_start", gb2[:, :], dt_["w_gb2"][:, :], inc=(ldg, 16))

    p.need("sync", p.tok(se))
    p.op("sync", "dma_start", lrow_bf[0:1, 0:1024],
         local[127:128, 15 * 1024:16 * 1024], inc=(ld, 16))
    p.op("sync", "dma_start", lrow_bf[0:1, 1024:2048],
         comp[127:128, 15 * 1024:16 * 1024], inc=(ld, 16))
    p.need("vector", p.tok(ld))
    p.op("vector", "tensor_copy", feats[0:1, :], lrow_bf[0:1, :], inc=(ve, 1))
    ln_row(p, feats[0:1, 0:1024], feats[0:1, 0:1024], gsq[0:1, 0:1024], gst,
           C.eps[0:1, 0:1], 1024, wrow=lnlw[0:1, :], brow=lnlb[0:1, :])
    p.need("sync", p.tok(ve))
    p.op("sync", "dma_start", bass.AP(dt_["d_gsc"], b * 2048, [[1, 2048]]),
         feats[0:1, :], inc=(st, 16))
    p.need("sync", p.tok(st))
    for j in range(16):
        p.op("sync", "dma_start", featC[:, j:j + 1],
             bass.AP(dt_["d_gsc"], b * 2048 + j * 128, [[1, 128], [1, 1]]),
             inc=(ld, 16))
    p.need("vector", p.tok(ld))
    p.op("vector", "tensor_copy", featCb[:, :], featC[:, :], inc=(ve, 1))
    p.need("tensor", p.tok(ve), p.tok(ldg))
    for half in range(2):
        for cj in range(16):
            p.op("tensor", "matmul", psG[0:1, half * 512:(half + 1) * 512],
                 featCb[:, cj:cj + 1],
                 gw1t[:, cj * 1024 + half * 512:cj * 1024 + (half + 1) * 512],
                 start=(cj == 0), stop=False)
        p.op("tensor", "matmul", psG[0:1, half * 512:(half + 1) * 512],
             C.ones_bf[0:1, 0:1], gb1[0:1, half * 512:(half + 1) * 512],
             start=False, stop=True, inc=(pe, 1))
    p.need("vector", p.tok(pe))
    p.op("vector", "tensor_copy", h1[0:1, :], psG[0:1, 0:1024], inc=(ve, 1))
    ln_row(p, grow[0:1, :], h1[0:1, :], gsq[0:1, 0:1024], gst, C.eps[0:1, 0:1],
           1024, wrow=glnw[0:1, :], brow=glnb[0:1, :])
    p.op("vector", "tensor_relu", h1[0:1, :], grow[0:1, :], inc=(ve, 1))
    p.need("sync", p.tok(ve))
    p.op("sync", "dma_start", bass.AP(dt_["d_gsc"], b * 2048, [[1, 1024]]),
         h1[0:1, :], inc=(st, 16))
    p.need("sync", p.tok(st))
    for j in range(8):
        p.op("sync", "dma_start", hC[:, j:j + 1],
             bass.AP(dt_["d_gsc"], b * 2048 + j * 128, [[1, 128], [1, 1]]),
             inc=(ld, 16))
    p.need("vector", p.tok(ld))
    p.op("vector", "tensor_copy", hCb[:, :], hC[:, :], inc=(ve, 1))
    p.need("tensor", p.tok(ve))
    for cj in range(8):
        p.op("tensor", "matmul", psG[0:1, 0:2], hCb[:, cj:cj + 1],
             gw2t[:, cj * 2:(cj + 1) * 2], start=(cj == 0), stop=False)
    p.op("tensor", "matmul", psG[0:1, 0:2], C.ones_bf[0:1, 0:1], gb2[0:1, :],
         start=False, stop=True, inc=(pe, 1))
    p.need("vector", p.tok(pe))
    p.opf("vector", "tensor_reduce", gst[0:1, 0:1], psG[0:1, 0:2], AX.X,
          ALU.max, negate=True)
    p.need("scalar", p.tok(ve))
    p.opf("scalar", "activation", grow[0:1, 0:2], psG[0:1, 0:2], AF.Exp,
          bias=gst[0:1, 0:1], accum_out=gst[0:1, 1:2])
    p.need("vector", p.tok(se))
    p.opf("vector", "reciprocal", gst[0:1, 2:3], gst[0:1, 1:2])
    p.opf("vector", "tensor_scalar", grow[0:1, 0:2], grow[0:1, 0:2],
          gst[0:1, 2:3], None, ALU.mult)
    p.need("tensor", p.tok(ve))
    p.op("tensor", "matmul", psG[:, 0:2], C.onesf_row[0:1, :],
         grow[0:1, 0:2], start=True, stop=True, inc=(pe, 1))
    p.need("vector", p.tok(pe))
    p.opf("vector", "tensor_copy", grep[:, 0:2], psG[:, 0:2])
    dbg_dump(p, dt_, "grep", grep, b=b)
    p.barrier()
    gctx.close()


def ln_row(p, out_ap, in_ap, sq_row, gst, eps_one, n, wrow=None, brow=None):
    ve = p.sem("ve")
    se = p.sem("se")
    p.opf("vector", "tensor_reduce", gst[0:1, 0:1], in_ap, AX.X, ALU.add)
    p.opf("vector", "tensor_scalar", gst[0:1, 0:1], gst[0:1, 0:1], 1.0 / n,
          None, ALU.mult)
    p.need("scalar", p.tok(ve))
    p.opf("scalar", "activation", sq_row, in_ap, AF.Square)
    p.need("vector", p.tok(se))
    p.opf("vector", "tensor_reduce", gst[0:1, 1:2], sq_row, AX.X, ALU.add)
    p.opf("vector", "tensor_scalar", gst[0:1, 1:2], gst[0:1, 1:2], 1.0 / n,
          None, ALU.mult)
    p.opf("vector", "tensor_tensor", gst[0:1, 2:3], gst[0:1, 0:1],
          gst[0:1, 0:1], ALU.mult)
    p.opf("vector", "tensor_sub", gst[0:1, 1:2], gst[0:1, 1:2], gst[0:1, 2:3])
    p.need("scalar", p.tok(ve))
    p.opf("scalar", "activation", gst[0:1, 2:3], gst[0:1, 1:2], AF.Ln,
          bias=eps_one)
    p.opf("scalar", "activation", gst[0:1, 2:3], gst[0:1, 2:3], AF.Exp,
          scale=-0.5)
    p.need("vector", p.tok(se))
    p.opf("vector", "tensor_scalar", out_ap, in_ap, gst[0:1, 0:1],
          gst[0:1, 2:3], ALU.subtract, ALU.mult)
    if wrow is not None:
        p.opf("vector", "tensor_mul", out_ap, out_ap, wrow)
        p.opf("vector", "tensor_add", out_ap, out_ap, brow)



def build_program(phases, w_shapes):
    nc = bass.Bass("TRN2", target_bir_lowering=False)
    ctx = ExitStack()
    p = P(nc, ctx)
    dt_ = declare_dram(nc, phases)
    consts = host_consts()
    for n, arr in consts.items():
        dty = BF16 if arr.dtype == BF else F32
        dt_["c_" + n] = nc.dram_tensor("c_" + n, list(arr.shape), dty,
                                       kind="ExternalInput")
    wnames = []
    for ph in phases:
        wnames += PHASE_WEIGHTS[ph]
    for n in wnames:
        dt_["w_" + n] = nc.dram_tensor("w_" + n, list(w_shapes[n]), F32,
                                       kind="ExternalInput")
    C = emit_consts(p, dt_, ctx)
    if "A" in phases:
        emit_phaseA(p, dt_, C)
    if "C" in phases:
        emit_phaseC(p, dt_, C)
    if "D" in phases:
        emit_phaseD(p, dt_, C)
    p.finish()
    ctx.close()
    return nc, consts, wnames


def make_inmaps(phases, inputs_prepped, consts, wnames, extra):
    base = {}
    for n, arr in consts.items():
        base["c_" + n] = np.ascontiguousarray(arr)
    for n in wnames:
        base["w_" + n] = np.ascontiguousarray(inputs_prepped[n].astype(np.float32))
    base.update(extra)
    return base


def kernel(**inputs):
    w = prep_weights(inputs)
    w_shapes = {n: w[n].shape for n in W_NAMES}
    x_full = np.asarray(inputs["L0_attn_x"])
    phases = ("A", "C", "D")
    nc, consts, wnames = build_program(phases, w_shapes)
    in_maps = []
    for core in range(8):
        m = make_inmaps(phases, w, consts, wnames,
                        {"x0": np.ascontiguousarray(
                            x_full[2 * core:2 * core + 2])})
        in_maps.append(m)
    res = run_bass_kernel_spmd(nc, in_maps, core_ids=list(range(8)))
    return np.concatenate([r["out"] for r in res.results], axis=0)


DI = 2048
DS = 128
NHD = 32
HD = 64
CONVD = 2304
NFP = 18
DIN = 4384
CH = 128
NC = 16
CHUNK_BARRIER = False
MIDBAR = 0
DUMPSEL = 15
DIRS = ("f", "r")


def emit_phaseC(p, dt_, C):
    ld = p.sem("ld")
    ldg = p.sem("ldg")
    st = p.sem("st")
    pe = p.sem("pe")
    ve = p.sem("ve")
    se = p.sem("se")

    with ExitStack() as cctx:
        WoutT = p.sb("WoutT", [128, 16 * 1024], BF16, cctx)
        tiled_w_load(p, "gpsimd", WoutT, dt_["w_WoutT"][:, :], 16,
                     p.sem("ldw"))
        reps = {}
        for d in ("f", "r"):
            reps["dtb_" + d] = p.sb("dtb" + d, [128, 32], F32, cctx)
            bcast_load(p, reps["dtb_" + d][:, :], dt_["w_dtb_" + d], 32, ld)
            reps["Alog_" + d] = p.sb("alog" + d, [128, 32], F32, cctx)
            bcast_load(p, reps["Alog_" + d][:, :], dt_["w_Alog_" + d], 32, ld)
            reps["Dsk_" + d] = p.sb("dsk" + d, [128, 32], F32, cctx)
            bcast_load(p, reps["Dsk_" + d][:, :], dt_["w_Dsk_" + d], 32, ld)
            reps["nw_" + d] = p.sb("nw" + d, [128, 2048], F32, cctx)
            bcast_load(p, reps["nw_" + d][:, :], dt_["w_nw_" + d], 2048, ld)
            cw1 = p.sb("cw1" + d, [128, NFP * 4], F32, cctx)
            src = dt_["w_conv1_" + d][:, :].rearrange("(f p) k -> p f k", p=128)
            p.op("sync", "dma_start",
                 cw1[:, :].rearrange("p (f k) -> p f k", f=NFP), src,
                 inc=(ld, 16))
            reps["cw1_" + d] = cw1
            cb1 = p.sb("cb1" + d, [128, NFP], F32, cctx)
            p.op("sync", "dma_start", cb1[:, :], dt_["w_conv1b_" + d][:, :],
                 inc=(ld, 16))
            reps["cb1_" + d] = cb1
        for d in ("f", "r"):
            negA = p.sb("negA" + d, [128, 32], F32, cctx)
            p.need("scalar", p.tok(ld))
            p.opf("scalar", "activation", negA[:, :], reps["Alog_" + d][:, :],
                  AF.Exp)
            p.need("vector", p.tok(se))
            p.opf("vector", "tensor_scalar", negA[:, :], negA[:, :], -1.0,
                  None, ALU.mult)
            reps["negA_" + d] = negA
            drepb = p.sb("drepb" + d, [128, 32], BF16, cctx)
            p.opf("vector", "tensor_copy", drepb[:, :], reps["Dsk_" + d][:, :])
            reps["Drep_" + d] = drepb
        rwmod = p.sb("rwmod", [128, 8 * 8], BF16, cctx)
        rsc8 = p.sb("rsc8", [128, 8], F32, cctx)
        rwt = p.sb("rwt", [128, 8 * 8], F32, cctx)
        p.op("sync", "dma_start",
             rwt[:, :].rearrange("p (dj e) -> p dj e", dj=8),
             dt_["w_rw"][:, :].rearrange("(dj p) e -> p dj e", p=128),
             inc=(ld, 16))
        p.op("sync", "dma_start", rsc8[:, :], dt_["w_rsc"][:, :],
             inc=(ld, 16))
        p.need("vector", p.tok(ld))
        p.opf("vector", "tensor_mul",
              rwmod[:, :].rearrange("p (dj e) -> p dj e", dj=8),
              rwt[:, :].rearrange("p (dj e) -> p dj e", dj=8),
              rsc8[:, :].unsqueeze(2).broadcast_to((128, 8, 8)))
        pes_rep = p.sb("pes_rep", [128, 8], F32, cctx)
        bcast_load(p, pes_rep[:, :], dt_["w_pes"], 8, ld)

        acro_rd = [None] * 4

        import os as _os
        _nb = int(_os.environ.get("NBATCH", str(B_PER_CORE)))
        for b in range(_nb):
            dt_sb = p.sb("dt_sb", [128, 16 * 32], F32, cctx)
            with ExitStack() as s1:
                emit_C1(p, dt_, C, b, dt_sb, s1)
            p.barrier()
            with ExitStack() as s2:
                attn_bufs = emit_ssd_alloc(p, s2)
                carried = {"a1": [None, None], "zc": [None, None],
                           "xw": None, "aw": [None],
                           "wout_ld": p.tok(p.sem("ldw"))}
                for d in DIRS:
                    emit_ssd_dir(p, dt_, C, b, d, dt_sb, reps, WoutT,
                                 attn_bufs, acro_rd, carried)
                    p.barrier()
            p.barrier()
            if _os.environ.get("SKIPC4", "0") != "1":
                with ExitStack() as s3:
                    emit_C4(p, dt_, C, b, rwmod, pes_rep, s3)
            p.barrier()


def emit_C1(p, dt_, C, b, dt_sb, s1):
    ld = p.sem("ld")
    ldg = p.sem("ldg")
    st = p.sem("st")
    pe = p.sem("pe")
    ve = p.sem("ve")

    x2T = p.sb("x2T", [128, 8 * T], BF16, s1)
    ldx2 = p.sem("ldx2")
    p.op("sync", "dma_start",
         x2T[:, :].rearrange("p (dj t) -> p dj t", dj=8),
         dt_["d_x2T"][b].rearrange("(dj p) t -> p dj t", p=128),
         inc=(ldx2, 16))
    wbuf = [p.sb(f"wc{i}", [128, 8 * 512], BF16, s1) for i in range(2)]
    zst = [p.sb(f"zst{i}", [128, 512], BF16, s1) for i in range(2)]
    psM = [p.ps(f"psC1{i}", [128, 512], F32, s1) for i in range(2)]

    wfree = [None, None]
    pfree = [None, None]
    zfree = [None, None]
    for fc in range(4):
        wb = wbuf[fc % 2]
        p.need("gpsimd", wfree[fc % 2])
        wt = tiled_w_load(p, "gpsimd", wb,
                          dt_["w_WinT"][:, fc * 512:(fc + 1) * 512], 8,
                          p.sem(f"ldg_w{fc % 2}"))
        for ti in range(NT):
            pm = psM[ti % 2]
            p.need("tensor", pfree[ti % 2], wt[0], p.tok(p.sem("ldx2")))
            for dj in range(8):
                p.op("tensor", "matmul", pm[:, :],
                     x2T[:, dj * T + ti * 128:dj * T + (ti + 1) * 128],
                     sb_ap(wb, dj * 512, [[1, 512]]),
                     start=(dj == 0), stop=(dj == 7),
                     inc=(pe, 1) if dj == 7 else None)
            p.need("vector", p.tok(pe), zfree[ti % 2])
            et = p.op("vector", "tensor_copy", zst[ti % 2][:, :], pm[:, :],
                      inc=(ve, 1))
            pfree[ti % 2] = et[0]
            p.need("sync", et[0])
            sz = p.op("sync", "dma_start",
                      dt_["d_z"][b, ti * 128:(ti + 1) * 128,
                                 fc * 512:(fc + 1) * 512],
                      zst[ti % 2][:, :], inc=(p.sem(f"st_z{ti % 2}"), 16))
            zfree[ti % 2] = sz[0]
        wfree[fc % 2] = p.tok(pe)
    wb = wbuf[0]
    p.need("gpsimd", wfree[0])
    wt = tiled_w_load(p, "gpsimd", wb, dt_["w_WinT"][:, 4352:4384], 8,
                      p.sem("ldg_w0"))
    for ti in range(NT):
        pm = psM[ti % 2]
        p.need("tensor", pfree[ti % 2], wt[0])
        for dj in range(8):
            p.op("tensor", "matmul", pm[:, 0:32],
                 x2T[:, dj * T + ti * 128:dj * T + (ti + 1) * 128],
                 sb_ap(wb, dj * 32, [[1, 32]]),
                 start=(dj == 0), stop=(dj == 7),
                 inc=(pe, 1) if dj == 7 else None)
        p.need("vector", p.tok(pe))
        et = p.op("vector", "tensor_copy", dt_sb[:, ti * 32:(ti + 1) * 32],
                  pm[:, 0:32], inc=(ve, 1))
        pfree[ti % 2] = et[0]
    wfree[0] = p.tok(pe)
    xst = [p.sb(f"xst{i}", [128, 512], BF16, s1) for i in range(2)]
    xfree = [None, None]
    for ft in range(NFP):
        wb = wbuf[ft % 2]
        p.need("gpsimd", wfree[ft % 2])
        wt = tiled_w_load(p, "gpsimd", wb,
                          dt_["w_WinT"][:, 2048 + ft * 128:2048 + (ft + 1) * 128],
                          8, p.sem(f"ldg_w{ft % 2}"))
        for tc in range(4):
            pm = psM[tc % 2]
            p.need("tensor", pfree[tc % 2], wt[0])
            for dj in range(8):
                p.op("tensor", "matmul", pm[:, :],
                     sb_ap(wb, dj * 128, [[1, 128]]),
                     x2T[:, dj * T + tc * 512:dj * T + (tc + 1) * 512],
                     start=(dj == 0), stop=(dj == 7),
                     inc=(pe, 1) if dj == 7 else None)
            p.need("vector", p.tok(pe), xfree[tc % 2])
            et = p.op("vector", "tensor_copy", xst[tc % 2][:, :], pm[:, :],
                      inc=(ve, 1))
            pfree[tc % 2] = et[0]
            p.need("sync", et[0])
            sx = p.op("sync", "dma_start",
                      dt_["d_xbc"][b, ft, :, tc * 512:(tc + 1) * 512],
                      xst[tc % 2][:, :], inc=(p.sem(f"st_x{tc % 2}"), 16))
            xfree[tc % 2] = sx[0]
        wfree[ft % 2] = p.tok(pe)
    p.need("sync", p.tok(st), p.tok(p.sem("st_z0")), p.tok(p.sem("st_z1")),
           p.tok(p.sem("st_x0")), p.tok(p.sem("st_x1")))


def emit_ssd_alloc(p, s2):
    bufs = {}
    bufs["xw"] = p.sb("xw", [128, NFP * 134], F32, s2)
    bufs["convtmp"] = p.sb("convtmp", [128, NFP * 128], F32, s2)
    bufs["xc"] = p.sb("xc", [128, NFP * 128], BF16, s2)
    bufs["xs"] = p.sb("xs_tm", [128, 2048], BF16, s2)
    bufs["Xd"] = p.sb("Xd", [128, 2048], BF16, s2)
    bufs["Xdd"] = p.sb("Xdd", [128, 2048], BF16, s2)
    bufs["dtc"] = p.sb("dtc", [128, 8 * 32], F32, s2)
    bufs["dtb16"] = p.sb("dtb16", [128, 32], BF16, s2)
    bufs["Acum"] = p.sb("Acum", [128, 32], F32, s2)
    bufs["decexp"] = p.sb("decexp", [128, 32], F32, s2)
    bufs["decb"] = p.sb("decb", [128, 32], BF16, s2)
    bufs["expAcum"] = p.sb("expAcum", [128, 32], F32, s2)
    bufs["expAtR"] = p.sb("expAtR", [128, 32], F32, s2)
    bufs["AcumT"] = p.sb("AcumT", [32, 128], F32, s2)
    bufs["AcRep"] = p.sb("AcRep", [128, 1024], F32, s2)
    bufs["Dm"] = p.sb("Dm", [128, 1024], F32, s2)
    bufs["Em"] = p.sb("Em", [128, 1024], BF16, s2)
    bufs["Mx"] = p.sb("Mx", [128, 1024], BF16, s2)
    bufs["Smb"] = p.sb("Smb", [128, 128], BF16, s2)
    bufs["Btm"] = p.sb("Btm", [128, 128], BF16, s2)
    bufs["run"] = p.sb("run", [128, 2048], F32, s2)
    bufs["rbf"] = p.sb("rbf", [128, 2048], BF16, s2)
    bufs["Ysb"] = p.sb("Ysb", [128, 2048], F32, s2)
    bufs["yt"] = p.sb("yt", [128, 2048], F32, s2)
    bufs["sqd"] = p.sb("sqd", [128, 2048], BF16, s2)
    bufs["gnb"] = p.sb("gnb", [128, 2048], BF16, s2)
    bufs["gnT"] = p.sb("gnT", [128, 2048], BF16, s2)
    bufs["zc"] = [p.sb(f"zc{i}", [128, 2048], BF16, s2) for i in range(2)]
    bufs["a1"] = [p.sb(f"a1_{i}", [128, 512], F32, s2) for i in range(2)]
    bufs["stat"] = p.sb("cstat", [128, 8], F32, s2)
    bufs["atot"] = p.sb("atot", [1, 32], F32, s2)
    bufs["Psm"] = p.ps("Psm", [128, 96], F32, s2)
    bufs["PsmR"] = p.ps("PsmR", [128, 32], F32, s2)
    bufs["PsmS"] = p.ps("PsmS", [128, 128], F32, s2)
    bufs["PsmT"] = p.ps("PsmT", [128, 128], F32, s2)
    bufs["Pbf"] = p.ps("Pbf", [128, 1024], BF16, s2)
    bufs["PY"] = p.ps("PY", [128, 512], F32, s2)
    bufs["PYo"] = p.ps("PYo", [128, 512], F32, s2)
    bufs["Pst"] = p.ps("Pst", [128, 64], F32, s2)
    return bufs


def emit_ssd_dir(p, dt_, C, b, d, dt_sb, reps, WoutT, B_, acro_rd, carried):
    ld = p.sem("ld")
    ldg = p.sem("ldg")
    st = p.sem("st")
    pe = p.sem("pe")
    ve = p.sem("ve")
    se = p.sem("se")
    fwd = (d == "f")
    cum = C.cum["cumI_f" if fwd else "cumI_r"]
    suf = C.cum["sufX_f" if fwd else "sufX_r"]
    umask = C.umask["umaskI_f" if fwd else "umaskI_r"]

    p.opf("vector", "memset", B_["run"][:, :], 0.0)
    run_tok = p.tok(p.sem("ve"))

    import os as _os
    _nch = int(_os.environ.get("NCHUNKS", str(NC)))
    chunks = list(range(_nch)) if fwd else list(range(NC - 1, NC - 1 - _nch, -1))
    aw_prev = carried["aw"]
    zc_free = carried["zc"]
    a1_free = carried["a1"]
    xw_free = carried["xw"]
    first = True
    for ci, c in enumerate(chunks):
        zb = B_["zc"][ci % 2]
        p.need("sync", zc_free[ci % 2])
        zl = p.op("sync", "dma_start", zb[:, :],
                  dt_["d_z"][b, c * 128:(c + 1) * 128, :],
                  inc=(p.sem(f"ld_zc{ci % 2}"), 16))
        lo = c * 128 - 3
        hi = c * 128 + 131
        clo = max(lo, 0)
        chi = min(hi, T)
        p.need("gpsimd", xw_free)
        if lo < 0 or hi > T:
            p.need("vector", xw_free)
            mz = p.opf("vector", "memset", B_["xw"][:, :], 0.0)
            p.need("gpsimd", mz[0])
        xo = clo - lo
        wl = p.op("gpsimd", "dma_start",
                  bass.AP(B_["xw"], xo,
                          [[NFP * 134, 128], [134, NFP], [1, chi - clo]]),
                  dt_["d_xbc"][b, :, :, clo:chi].transpose([1, 0, 2]),
                  inc=(p.sem("ldg"), 16))
        dtc = B_["dtc"]
        u = dtc[:, 0:32]
        p.need("vector", p.tok(ld))
        p.opf("vector", "tensor_add", u, dt_sb[:, c * 32:(c + 1) * 32],
              reps["dtb_" + d][:, :])
        p.need("scalar", p.tok(ve))
        p.opf("scalar", "activation", dtc[:, 32:64], u, AF.Abs)
        p.opf("scalar", "activation", dtc[:, 64:96], dtc[:, 32:64], AF.Exp,
              scale=-1.0)
        p.opf("scalar", "activation", dtc[:, 96:128], dtc[:, 64:96], AF.Ln,
              bias=C.one_col[:, 0:1])
        p.need("vector", p.tok(se))
        p.opf("vector", "tensor_relu", dtc[:, 128:160], u)
        p.opf("vector", "tensor_add", dtc[:, 160:192], dtc[:, 128:160],
              dtc[:, 96:128])
        p.opf("vector", "tensor_mul", dtc[:, 192:224], dtc[:, 160:192],
              reps["negA_" + d][:, :])
        p.opf("vector", "tensor_copy", B_["dtb16"][:, :], dtc[:, 160:192])
        dtA = dtc[:, 192:224]
        dt_tok = p.tok(p.sem("ve"))
        Psm = B_["Psm"]
        p.need("tensor", dt_tok, p.tok(ve))
        p.op("tensor", "matmul", Psm[:, 0:32], cum[:, :], dtA, start=True,
             stop=True)
        p.op("tensor", "matmul", Psm[:, 32:64], suf[:, :], dtA, start=True,
             stop=True)
        p.op("tensor", "matmul", Psm[0:1, 64:96], C.onesf_col[:, 0:1], dtA,
             start=True, stop=True, inc=(pe, 1))
        p.need("vector", p.tok(pe))
        p.opf("vector", "tensor_copy", B_["Acum"][:, :], Psm[:, 0:32])
        p.opf("vector", "tensor_copy", B_["decexp"][:, :], Psm[:, 32:64])
        p.opf("vector", "tensor_copy", B_["atot"][0:1, :], Psm[0:1, 64:96])
        acum_tok = p.tok(p.sem("ve"))
        p.need("scalar", acum_tok)
        p.opf("scalar", "activation", B_["decexp"][:, :], B_["decexp"][:, :],
              AF.Exp)
        p.opf("scalar", "activation", B_["expAcum"][:, :], B_["Acum"][:, :],
              AF.Exp)
        p.opf("scalar", "activation", B_["atot"][0:1, :], B_["atot"][0:1, :],
              AF.Exp)
        sc_tok = p.tok(p.sem("se"))
        p.need("vector", sc_tok)
        p.opf("vector", "tensor_copy", B_["decb"][:, :], B_["decexp"][:, :])
        p.need("tensor", sc_tok)
        p.op("tensor", "matmul", B_["PsmR"][:, :], C.onesf_row[0:1, :],
             B_["atot"][0:1, :], start=True, stop=True, inc=(pe, 1))
        p.need("vector", p.tok(pe))
        p.opf("vector", "tensor_copy", B_["expAtR"][:, :], B_["PsmR"][:, :])
        p.need("tensor", acum_tok)
        p.op("tensor", "transpose", B_["PsmT"][0:32, :], B_["Acum"][:, :],
             C.identf[:, :], inc=(pe, 1))
        p.need("vector", p.tok(pe), aw_prev[0])
        p.opf("vector", "tensor_copy", B_["AcumT"][0:32, :],
              B_["PsmT"][0:32, :])
        slot = (b * 32 + (0 if fwd else NC) + ci) % 4
        p.need("sync", p.tok(ve), acro_rd[slot])
        aw = p.op("sync", "dma_start",
                  bass.AP(dt_["d_acrow"], slot * 4096, [[128, 32], [1, 128]]),
                  B_["AcumT"][0:32, :], inc=(p.sem("st_aw"), 16))
        aw_tok = aw[0]
        aw_prev[0] = aw_tok
        if MIDBAR & 4:
            p.barrier()
        p.need("vector", wl[0], p.tok(p.sem("ldg")))
        cw1 = reps["cw1_" + d]
        cb1 = reps["cb1_" + d]
        silu_toks = []
        for ft in range(NFP):
            base = ft * 134
            ks = [0, 1, 2, 3] if fwd else [6, 5, 4, 3]
            acc = B_["convtmp"][:, ft * 128:(ft + 1) * 128]
            p.op("vector", "tensor_scalar", acc,
                 B_["xw"][:, base + ks[0]:base + ks[0] + 128],
                 cw1[:, ft * 4 + 0:ft * 4 + 1], None, ALU.mult)
            for k in (1, 2):
                p.op("vector", "scalar_tensor_tensor", acc,
                     B_["xw"][:, base + ks[k]:base + ks[k] + 128],
                     cw1[:, ft * 4 + k:ft * 4 + k + 1], acc, ALU.mult, ALU.add)
            p.op("vector", "scalar_tensor_tensor", acc,
                 B_["xw"][:, base + ks[3]:base + ks[3] + 128],
                 cw1[:, ft * 4 + 3:ft * 4 + 4], acc, ALU.mult, ALU.add,
                 inc=(ve, 1))
            p.need("scalar", p.tok(ve))
            stk = p.opf("scalar", "activation",
                        B_["xc"][:, ft * 128:(ft + 1) * 128], acc, AF.Silu,
                        bias=cb1[:, ft:ft + 1])
            silu_toks.append(stk[0])
        xw_free = silu_toks[-1]
        conv_tok = silu_toks[-1]
        if False and (DUMPSEL & 1) and c in (0, 1):
            dbg_dump(p, dt_, f"xc_{d}_{c}", B_["xc"], b=b)
            dbg_dump(p, dt_, f"dt_{d}_{c}", B_["dtc"], b=b)
        if MIDBAR & 32:
            pseudo_dump(p, dt_, B_["xc"])
        if MIDBAR & 1:
            p.barrier()
        Pbf = B_["Pbf"]
        for rnd in range(2):
            p.need("tensor", conv_tok, p.tok(ve))
            for j in range(8):
                jj = rnd * 8 + j
                p.op("tensor", "transpose", Pbf[:, j * 128:(j + 1) * 128],
                     B_["xc"][:, jj * 128:(jj + 1) * 128], C.ident[:, :],
                     inc=(pe, 1) if j == 7 else None)
            p.need("vector", p.tok(pe))
            p.opf("vector", "tensor_copy",
                  B_["xs"][:, rnd * 1024:(rnd + 1) * 1024], Pbf[:, :])
        p.opf("vector", "tensor_mul",
              B_["Xd"][:, :].rearrange("p (h q) -> p h q", h=32),
              B_["xs"][:, :].rearrange("p (h q) -> p h q", h=32),
              B_["dtb16"][:, :].unsqueeze(2).broadcast_to((128, 32, 64)))
        p.opf("vector", "tensor_mul",
              B_["Xdd"][:, :].rearrange("p (h q) -> p h q", h=32),
              B_["Xd"][:, :].rearrange("p (h q) -> p h q", h=32),
              B_["decb"][:, :].unsqueeze(2).broadcast_to((128, 32, 64)))
        xd_tok = p.tok(p.sem("ve"))
        Bt = B_["xc"][:, 16 * 128:17 * 128]
        Ct = B_["xc"][:, 17 * 128:18 * 128]
        p.need("tensor", conv_tok, p.tok(ve))
        p.op("tensor", "matmul", B_["PsmS"][:, :], Bt, Ct,
             start=True, stop=True, inc=(pe, 1))
        p.need("vector", p.tok(pe))
        p.opf("vector", "tensor_mul", B_["Smb"][:, :], B_["PsmS"][:, :],
              umask[:, :])
        smb_tok = p.tok(p.sem("ve"))
        p.need("tensor", p.tok(ve))
        p.op("tensor", "transpose", Pbf[:, 0:128], Bt, C.ident[:, :],
             inc=(pe, 1))
        p.need("vector", p.tok(pe))
        p.opf("vector", "tensor_copy", B_["Btm"][:, :], Pbf[:, 0:128])
        btm_tok = p.tok(p.sem("ve"))
        p.need("vector", run_tok)
        p.opf("vector", "tensor_copy", B_["rbf"][:, :], B_["run"][:, :])
        rbf_tok = p.tok(p.sem("ve"))
        PY = B_["PY"]
        PYo = B_["PYo"]
        Pst = B_["Pst"]
        py_free = None
        pyo_free = None
        pst_free = None
        for g in range(4):
            p.need("sync", aw_tok, p.tok(ve))
            ar = p.op("sync", "dma_start", B_["AcRep"][:, :],
                      bass.AP(dt_["d_acrow"], slot * 4096 + g * 1024,
                              [[0, 128], [1, 1024]]), inc=(ld, 16))
            acro_rd[slot] = ar[0]
            p.need("vector", ar[0], acum_tok)
            p.opf("vector", "tensor_sub",
                  B_["Dm"][:, :].rearrange("p (h l) -> p h l", h=8),
                  B_["AcRep"][:, :].rearrange("p (h l) -> p h l", h=8),
                  B_["Acum"][:, g * 8:(g + 1) * 8].unsqueeze(2)
                  .broadcast_to((128, 8, 128)))
            p.opf("vector", "tensor_scalar_min", B_["Dm"][:, :],
                  B_["Dm"][:, :], 0.0)
            p.need("scalar", p.tok(ve))
            p.opf("scalar", "activation", B_["Em"][:, :], B_["Dm"][:, :],
                  AF.Exp)
            p.need("vector", p.tok(se), smb_tok)
            p.opf("vector", "tensor_mul",
                  B_["Mx"][:, :].rearrange("p (h l) -> p h l", h=8),
                  B_["Em"][:, :].rearrange("p (h l) -> p h l", h=8),
                  B_["Smb"][:, :].unsqueeze(1).broadcast_to((128, 8, 128)))
            mx_tok = p.tok(p.sem("ve"))
            upd_toks = []
            for hh in range(8):
                h = g * 8 + hh
                p.need("tensor", mx_tok, xd_tok, py_free)
                p.op("tensor", "matmul", PY[:, hh * 64:(hh + 1) * 64],
                     B_["Mx"][:, hh * 128:(hh + 1) * 128],
                     B_["Xd"][:, h * 64:(h + 1) * 64], start=True, stop=True)
                p.need("tensor", rbf_tok, pyo_free)
                p.op("tensor", "matmul", PYo[:, hh * 64:(hh + 1) * 64],
                     Ct, B_["rbf"][:, h * 64:(h + 1) * 64],
                     start=True, stop=True,
                     inc=(pe, 1) if hh == 7 else None)
                p.need("tensor", btm_tok, pst_free)
                ps_mm = p.op("tensor", "matmul", Pst[:, :], B_["Btm"][:, :],
                             B_["Xdd"][:, h * 64:(h + 1) * 64], start=True,
                             stop=True, inc=(pe, 1))
                p.need("vector", ps_mm[0], sc_tok)
                ut = p.opf("vector", "scalar_tensor_tensor",
                           B_["run"][:, h * 64:(h + 1) * 64],
                           B_["run"][:, h * 64:(h + 1) * 64],
                           B_["expAtR"][:, h:h + 1], Pst[:, :],
                           ALU.mult, ALU.add)
                pst_free = ut[0]
                upd_toks.append(ut[0])
            run_tok = upd_toks[-1]
            p.need("vector", p.tok(pe), sc_tok)
            p.opf("vector", "tensor_mul",
                  B_["yt"][:, g * 512:(g + 1) * 512]
                  .rearrange("p (h q) -> p h q", h=8),
                  PYo[:, :].rearrange("p (h q) -> p h q", h=8),
                  B_["expAcum"][:, g * 8:(g + 1) * 8].unsqueeze(2)
                  .broadcast_to((128, 8, 64)))
            yadd = p.opf("vector", "tensor_add",
                         B_["Ysb"][:, g * 512:(g + 1) * 512],
                         B_["yt"][:, g * 512:(g + 1) * 512], PY[:, :])
            py_free = yadd[0]
            pyo_free = yadd[0]
        if False and (DUMPSEL & 4) and c in (0, 1):
            dbg_dump(p, dt_, f"ynod_{d}_{c}", B_["Ysb"], b=b)
        if MIDBAR & 32:
            pseudo_dump(p, dt_, B_["Ysb"])
        if MIDBAR & 2:
            p.barrier()
        p.need("vector", p.tok(ve))
        p.opf("vector", "tensor_mul",
              B_["yt"][:, :].rearrange("p (h q) -> p h q", h=32),
              B_["xs"][:, :].rearrange("p (h q) -> p h q", h=32),
              reps["Drep_" + d][:, :].unsqueeze(2).broadcast_to((128, 32, 64)))
        p.opf("vector", "tensor_add", B_["Ysb"][:, :], B_["Ysb"][:, :],
              B_["yt"][:, :])
        if False and (DUMPSEL & 2) and c in (0, 1):
            dbg_dump(p, dt_, f"ypre_{d}_{c}", B_["Ysb"], b=b)
            dbg_dump(p, dt_, f"smb_{d}_{c}", B_["Smb"], b=b)
            dbg_dump(p, dt_, f"mx_{d}_{c}", B_["Mx"], b=b)
            dbg_dump(p, dt_, f"xd_{d}_{c}", B_["Xd"], b=b)
        p.need("scalar", zl[0], p.tok(ld), p.tok(p.sem("ve")))
        p.opf("scalar", "activation", B_["yt"][:, :], zb[:, :], AF.Silu)
        zc_free[ci % 2] = p.tok(p.sem("se"))
        p.need("vector", p.tok(se))
        p.opf("vector", "tensor_mul", B_["Ysb"][:, :], B_["Ysb"][:, :],
              B_["yt"][:, :])
        p.need("scalar", p.tok(ve))
        p.opf("scalar", "activation", B_["sqd"][:, :], B_["Ysb"][:, :],
              AF.Square, accum_out=B_["stat"][:, 0:1])
        p.need("vector", p.tok(se))
        p.opf("vector", "tensor_scalar", B_["stat"][:, 1:2], B_["stat"][:, 0:1],
              1.0 / 2048, None, ALU.mult)
        p.need("scalar", p.tok(ve))
        p.opf("scalar", "activation", B_["stat"][:, 2:3], B_["stat"][:, 1:2],
              AF.Ln, bias=C.eps[:, 0:1])
        p.opf("scalar", "activation", B_["stat"][:, 2:3], B_["stat"][:, 2:3],
              AF.Exp, scale=-0.5)
        p.need("vector", p.tok(se))
        p.opf("vector", "tensor_scalar", B_["Ysb"][:, :], B_["Ysb"][:, :],
              B_["stat"][:, 2:3], None, ALU.mult)
        p.opf("vector", "tensor_mul", B_["gnb"][:, :], B_["Ysb"][:, :],
              reps["nw_" + d][:, :])
        gn_tok = p.tok(p.sem("ve"))
        if False and (DUMPSEL & 8) and c in (0, 1):
            dbg_dump(p, dt_, f"gnb_{d}_{c}", B_["gnb"], b=b)
            dbg_dump(p, dt_, f"acum_{d}_{c}", B_["Acum"], b=b)
        if MIDBAR & 8:
            p.barrier()
        if MIDBAR & 32:
            pseudo_dump(p, dt_, B_["gnb"])
        for rnd in range(2):
            p.need("tensor", gn_tok, p.tok(ve))
            for j in range(8):
                jj = rnd * 8 + j
                p.op("tensor", "transpose", Pbf[:, j * 128:(j + 1) * 128],
                     B_["gnb"][:, jj * 128:(jj + 1) * 128], C.ident[:, :],
                     inc=(pe, 1) if j == 7 else None)
            p.need("vector", p.tok(pe))
            p.opf("vector", "tensor_copy",
                  B_["gnT"][:, rnd * 1024:(rnd + 1) * 1024], Pbf[:, :])
        gnT_tok = p.tok(p.sem("ve"))
        for fc in range(2):
            p.need("tensor", gnT_tok, a1_free[fc], py_free, pyo_free,
                   carried["wout_ld"])
            for j in range(16):
                p.op("tensor", "matmul", B_["PY" if fc == 0 else "PYo"][:, :],
                     B_["gnT"][:, j * 128:(j + 1) * 128],
                     WoutT[:, j * 1024 + fc * 512:j * 1024 + (fc + 1) * 512],
                     start=(j == 0), stop=(j == 15),
                     inc=(pe, 1) if j == 15 else None)
            p.need("vector", p.tok(pe), a1_free[fc])
            ev = p.opf("vector", "tensor_copy", B_["a1"][fc][:, :],
                       B_["PY" if fc == 0 else "PYo"][:, :])
            py_free = ev[0] if fc == 0 else py_free
            pyo_free = ev[0] if fc == 1 else pyo_free
            if False and (DUMPSEL & 16) and c in (0,) and fc == 0:
                dbg_dump(p, dt_, f"a1_{d}_{c}", B_["a1"][fc], b=b)
                dbg_dump(p, dt_, f"gnT_{d}_{c}", B_["gnT"], b=b)
            if MIDBAR & 16:
                p.barrier()
            p.need("gpsimd", ev[0])
            stg = p.sem(f"stg{fc}")
            if fwd:
                sa = p.op("gpsimd", "dma_start",
                          dt_["d_attn1"][b, c * 128:(c + 1) * 128,
                                         fc * 512:(fc + 1) * 512],
                          B_["a1"][fc][:, :], inc=(stg, 16))
            else:
                sa = p.op("gpsimd", "dma_start",
                          dt_["d_attn1"][b, c * 128:(c + 1) * 128,
                                         fc * 512:(fc + 1) * 512],
                          B_["a1"][fc][:, :], accum_op=ALU.add, inc=(stg, 16))
            a1_free[fc] = sa[0]
        if CHUNK_BARRIER:
            p.barrier()
        first = False
    carried["xw"] = xw_free
    p.need("sync", p.tok(st))
    p.need("gpsimd", p.tok(p.sem("stg0")), p.tok(p.sem("stg1")))
    if False:
        dbg_dump(p, dt_, f"fin_a1f0_{d}", B_["a1"][0], b=b)
        dbg_dump(p, dt_, f"fin_a1f1_{d}", B_["a1"][1], b=b)
        dbg_dump(p, dt_, f"fin_gnb_{d}", B_["gnb"], b=b)
        dbg_dump(p, dt_, f"fin_gnT_{d}", B_["gnT"], b=b)


def emit_C4(p, dt_, C, b, rwmod, pes_rep, s3):
    ld = p.sem("ld")
    st = p.sem("st")
    pe = p.sem("pe")
    ve = p.sem("ve")
    se = p.sem("se")

    x2b = [p.sb(f"c4x2{i}", [128, 1024], BF16, s3) for i in range(2)]
    at1 = [p.sb(f"c4at{i}", [128, 1024], F32, s3) for i in range(2)]
    x3t = p.sb("c4x3t", [128, 1024], F32, s3)
    x3b = p.sb("c4x3b", [128, 1024], BF16, s3)
    x3Tb = p.sb("c4x3T", [128, 1024], BF16, s3)
    sq = p.sb("c4sq", [128, 1024], F32, s3)
    stats = p.sb("c4st", [128, 8], F32, s3)
    lg = p.sb("c4lg", [128, 8], F32, s3)
    q8 = p.sb("c4q8", [128, 8], F32, s3)
    comb = p.sb("c4comb", [128, 8], F32, s3)
    psT = p.ps("c4psT", [128, 1024], BF16, s3)
    psL = p.ps("c4psL", [128, 8], F32, s3)

    ld_free = [None, None]
    psT_free = None
    x3b_free = None
    x3T_free = None
    comb_free = None
    for ti in range(NT):
        p.need("sync", ld_free[ti % 2])
        l1 = p.op("sync", "dma_start", x2b[ti % 2][:, :],
                  dt_["d_x2"][b, ti * 128:(ti + 1) * 128, :],
                  inc=(p.sem(f"ld_c4{ti % 2}"), 16))
        l2 = p.op("sync", "dma_start", at1[ti % 2][:, :],
                  dt_["d_attn1"][b, ti * 128:(ti + 1) * 128, :],
                  inc=(p.sem(f"ld_c4{ti % 2}"), 16))
        p.need("vector", l1[0], l2[0])
        p.opf("vector", "tensor_copy", x3t[:, :], x2b[ti % 2][:, :])
        ad = p.opf("vector", "tensor_add", x3t[:, :], x3t[:, :],
                   at1[ti % 2][:, :])
        ld_free[ti % 2] = ad[0]
        p.need("vector", x3b_free)
        ln_tile(p, x3t[:, :], x3t[:, :], sq[:, :], stats, C.eps, 1024,
                out2=x3b[:, :])
        p.need("sync", p.tok(ve))
        s1 = p.op("sync", "dma_start", dt_["d_x3"][b, ti * 128:(ti + 1) * 128, :],
                  x3b[:, :], inc=(p.sem("st_x3"), 16))
        x3b_free = s1[0]
        p.need("tensor", p.tok(ve), psT_free)
        for dj in range(8):
            p.op("tensor", "transpose", psT[:, dj * 128:(dj + 1) * 128],
                 x3b[:, dj * 128:(dj + 1) * 128], C.ident[:, :],
                 inc=(pe, 1) if dj == 7 else None)
        p.need("vector", p.tok(pe), x3T_free)
        et = p.opf("vector", "tensor_copy", x3Tb[:, :], psT[:, :])
        psT_free = et[0]
        p.need("sync", et[0])
        s2 = p.op("sync", "dma_start",
                  bass.AP(dt_["d_x3T"], b * D * T + ti * 128,
                          [[T, 128], [128 * T, 8], [1, 128]]),
                  x3Tb[:, :].rearrange("p (dj c) -> p dj c", dj=8),
                  inc=(p.sem("st_x3T"), 16))
        x3T_free = s2[0]
        p.need("scalar", p.tok(ve))
        p.opf("scalar", "activation", sq[:, :], x3t[:, :], AF.Square,
              accum_out=stats[:, 4:5])
        p.need("vector", p.tok(se))
        p.opf("vector", "tensor_scalar", stats[:, 5:6], stats[:, 4:5],
              1.0 / 1024, None, ALU.mult)
        p.need("scalar", p.tok(ve))
        p.opf("scalar", "activation", stats[:, 6:7], stats[:, 5:6], AF.Ln,
              bias=C.eps6[:, 0:1])
        p.opf("scalar", "activation", stats[:, 6:7], stats[:, 6:7], AF.Exp,
              scale=-0.5)
        p.need("vector", p.tok(se))
        p.opf("vector", "tensor_scalar", stats[:, 6:7], stats[:, 6:7],
              1.0 / 32.0, None, ALU.mult)
        sc_tok = p.tok(p.sem("ve"))
        p.need("tensor", et[0], p.tok(ve))
        for dj in range(8):
            p.op("tensor", "matmul", psL[:, 0:8],
                 x3Tb[:, dj * 128:(dj + 1) * 128], rwmod[:, dj * 8:(dj + 1) * 8],
                 start=(dj == 0), stop=(dj == 7), inc=(pe, 1) if dj == 7 else None)
        p.need("vector", p.tok(pe), sc_tok, comb_free)
        p.opf("vector", "tensor_scalar", lg[:, :], psL[:, 0:8],
              stats[:, 6:7], None, ALU.mult)
        p.opf("vector", "tensor_reduce", stats[:, 7:8], lg[:, :], AX.X,
              ALU.max, negate=True)
        p.need("scalar", p.tok(ve))
        p.opf("scalar", "activation", q8[:, :], lg[:, :], AF.Exp,
              bias=stats[:, 7:8])
        p.need("vector", p.tok(se))
        p.opf("vector", "tensor_reduce", stats[:, 0:1], q8[:, :], AX.X,
              ALU.max)
        p.opf("vector", "tensor_scalar", comb[:, :], q8[:, :], stats[:, 0:1],
              None, ALU.is_lt)
        p.opf("vector", "tensor_mul", comb[:, :], comb[:, :], q8[:, :])
        p.opf("vector", "tensor_reduce", stats[:, 1:2], comb[:, :], AX.X,
              ALU.max)
        p.opf("vector", "tensor_scalar", comb[:, :], q8[:, :], stats[:, 1:2],
              None, ALU.is_ge)
        p.opf("vector", "tensor_mul", comb[:, :], comb[:, :], q8[:, :])
        p.opf("vector", "tensor_add", stats[:, 2:3], stats[:, 0:1],
              stats[:, 1:2])
        p.opf("vector", "reciprocal", stats[:, 3:4], stats[:, 2:3])
        p.opf("vector", "tensor_scalar", comb[:, :], comb[:, :],
              stats[:, 3:4], None, ALU.mult)
        cb = p.opf("vector", "tensor_mul", comb[:, :], comb[:, :],
                   pes_rep[:, :])
        p.need("sync", cb[0])
        s3_ = p.op("sync", "dma_start",
                   dt_["d_comb"][b, ti * 128:(ti + 1) * 128, :], comb[:, :],
                   inc=(p.sem("st_cb"), 16))
        comb_free = s3_[0]
    p.need("sync", p.tok(st), p.tok(p.sem("st_x3")), p.tok(p.sem("st_x3T")),
           p.tok(p.sem("st_cb")))


EXPERT_BARRIER = False
TAIL_BARRIER = False
SKIP_LN = False
GA = 0.044715
GC = 0.7978845608028654


def emit_phaseD(p, dt_, C):
    ld = p.sem("ld")
    ldg = p.sem("ldg")
    st = p.sem("st")
    pe = p.sem("pe")
    ve = p.sem("ve")
    se = p.sem("se")

    with ExitStack() as dctx:
        x3T = p.sb("d_x3Ts", [128, 8 * T], BF16, dctx)
        comb_sb = p.sb("d_combs", [128, 16 * 8], F32, dctx)
        act = p.sb("d_act", [128, 16 * 1024], BF16, dctx)
        lin_sb = p.sb("d_lin", [128, 16 * 1024], BF16, dctx)
        oacc = p.sb("d_oacc", [128, 8 * 1024], F32, dctx)
        wg = [p.sb(f"d_wg{i}", [128, 8 * 256], BF16, dctx) for i in range(2)]
        xh = p.sb("d_xh", [128, 1024], BF16, dctx)
        g1s = p.sb("d_g1s", [128, 1024], BF16, dctx)
        tb = p.sb("d_tb", [128, 1024], BF16, dctx)
        ub = p.sb("d_ub", [128, 1024], BF16, dctx)
        sg = p.sb("d_sg", [128, 1024], BF16, dctx)
        x3row = [p.sb(f"d_x3r{i}", [128, 1024], BF16, dctx) for i in range(2)]
        x4t = p.sb("d_x4", [128, 1024], F32, dctx)
        sq = p.sb("d_sq", [128, 1024], F32, dctx)
        stats = p.sb("d_st", [128, 4], F32, dctx)
        PG0 = p.ps("PG0", [128, 1024], F32, dctx)
        PG1 = p.ps("PG1", [128, 1024], F32, dctx)
        PO = p.ps("PO", [128, 1024], F32, dctx)

        wg_free = [None, None]
        pg_free = None
        pg_free = []
        po_free = None
        act_free = None
        lin_free = None
        x3r_free = [None, None]
        for b in range(B_PER_CORE):
            ldx = p.sem("ldx")
            lx = p.op("sync", "dma_start",
                      x3T[:, :].rearrange("p (dj t) -> p dj t", dj=8),
                      dt_["d_x3T"][b].rearrange("(dj p) t -> p dj t", p=128),
                      inc=(ldx, 16))
            cbl = p.op("sync", "dma_start",
                       comb_sb[:, :].rearrange("p (ti e) -> p ti e", ti=16),
                       dt_["d_comb"][b].rearrange("(ti p) e -> p ti e", p=128),
                       inc=(ld, 16))
            bload = p.tok(ld)
            xload = p.tok(p.sem("ldx"))
            for th in range(2):
                mz = p.opf("vector", "memset", oacc[:, :], 0.0)
                for e in range(8):
                    p.need("gpsimd", lin_free)
                    ll = tiled_w_load(p, "gpsimd", lin_sb,
                                      dt_["w_lin"][e], 16, p.sem("ldl"))
                    for hc in range(16):
                        wb = wg[hc % 2]
                        p.need("gpsimd", wg_free[hc % 2])
                        wt = tiled_w_load(p, "gpsimd", wb,
                                          dt_["w_WgT"][e, hc], 8,
                                          p.sem(f"ldg_wg{hc % 2}"))
                        if pg_free:
                            p.need("tensor", wt[0], bload, xload, *pg_free)
                        else:
                            p.need("tensor", wt[0], bload, xload)
                        for half in range(2):
                            for dj in range(8):
                                p.op("tensor", "matmul",
                                     PG0[:, half * 512:(half + 1) * 512],
                                     sb_ap(wb, dj * 256, [[1, 128]]),
                                     x3T[:, dj * T + th * 1024 + half * 512:
                                         dj * T + th * 1024 + (half + 1) * 512],
                                     start=(dj == 0), stop=(dj == 7))
                            for dj in range(8):
                                p.op("tensor", "matmul",
                                     PG1[:, half * 512:(half + 1) * 512],
                                     sb_ap(wb, dj * 256 + 128, [[1, 128]]),
                                     x3T[:, dj * T + th * 1024 + half * 512:
                                         dj * T + th * 1024 + (half + 1) * 512],
                                     start=(dj == 0), stop=(dj == 7),
                                     inc=(pe, 1) if (dj == 7 and half == 1)
                                     else None)
                        gm_tok = p.tok(pe)
                        p.need("scalar", gm_tok, p.tok(ve))
                        p.opf("scalar", "activation", xh[:, :], PG0[:, :],
                              AF.Copy, scale=0.5)
                        p.need("vector", p.tok(se), gm_tok)
                        p.opf("vector", "tensor_copy", g1s[:, :], PG1[:, :])
                        pg_free = [p.tok(p.sem("ve")), p.tok(p.sem("se"))]
                        p.opf("vector", "tensor_mul", tb[:, :], xh[:, :],
                              xh[:, :])
                        p.opf("vector", "tensor_scalar", tb[:, :], tb[:, :],
                              4.0 * GA, 1.0, ALU.mult, ALU.add)
                        p.opf("vector", "tensor_mul", ub[:, :], tb[:, :],
                              xh[:, :])
                        p.need("scalar", p.tok(ve))
                        p.opf("scalar", "activation", sg[:, :], ub[:, :],
                              AF.Sigmoid, scale=4.0 * GC)
                        p.need("vector", p.tok(se), act_free)
                        p.opf("vector", "tensor_mul", tb[:, :], sg[:, :],
                              xh[:, :])
                        p.opf("vector", "scalar_tensor_tensor",
                              act[:, hc * 1024:(hc + 1) * 1024], tb[:, :],
                              2.0, g1s[:, :], ALU.mult, ALU.mult)
                        wg_free[hc % 2] = p.tok(pe)
                    act_tok = p.tok(p.sem("ve"))
                    for tr in range(8):
                        ti = th * 8 + tr
                        p.need("tensor", act_tok, ll[0], po_free)
                        for dh in range(2):
                            for hc in range(16):
                                p.op("tensor", "matmul",
                                     PO[:, dh * 512:(dh + 1) * 512],
                                     act[:, hc * 1024 + tr * 128:
                                         hc * 1024 + (tr + 1) * 128],
                                     sb_ap(lin_sb, hc * 1024 + dh * 512,
                                           [[1, 512]]),
                                     start=(hc == 0), stop=(hc == 15),
                                     inc=(pe, 1) if (hc == 15 and dh == 1)
                                     else None)
                        p.need("vector", p.tok(pe))
                        ev = p.opf("vector", "scalar_tensor_tensor",
                                   oacc[:, tr * 1024:(tr + 1) * 1024],
                                   PO[:, :],
                                   comb_sb[:, ti * 8 + e:ti * 8 + e + 1],
                                   oacc[:, tr * 1024:(tr + 1) * 1024],
                                   ALU.mult, ALU.add)
                        po_free = ev[0]
                    act_free = p.tok(pe)
                    lin_free = p.tok(pe)
                    if False and th == 0:
                        p.need("sync", p.tok(ve))
                        dbg_dump_nb(p, dt_, f"oe_{b}_{e}", oacc, b=b)
                        if e == 0:
                            dbg_dump_nb(p, dt_, f"acte_{b}", act, b=b)
                    if EXPERT_BARRIER:
                        p.barrier()
                if DBG:
                    dbg_dump(p, dt_, f"oacc_{b}_{th}", oacc, b=b)
                for tr in range(8):
                    ti = th * 8 + tr
                    p.need("sync", x3r_free[tr % 2])
                    l3 = p.op("sync", "dma_start", x3row[tr % 2][:, :],
                              dt_["d_x3"][b, ti * 128:(ti + 1) * 128, :],
                              inc=(p.sem(f"ld_x3r{tr % 2}"), 16))
                    p.need("vector", l3[0], p.tok(ve))
                    p.opf("vector", "tensor_copy", x4t[:, :],
                          x3row[tr % 2][:, :])
                    ad = p.opf("vector", "tensor_add", x4t[:, :], x4t[:, :],
                               oacc[:, tr * 1024:(tr + 1) * 1024])
                    x3r_free[tr % 2] = ad[0]
                    if DBG:
                        p.need("sync", p.tok(ve))
                        dbg_dump_nb(p, dt_, f"tx3_{b}_{th}_{tr}",
                                       x3row[tr % 2], b=b)
                        dbg_dump_nb(p, dt_, f"tx4_{b}_{th}_{tr}", x4t, b=b)
                    if TAIL_BARRIER:
                        p.barrier()
                    if False and th == 0 and tr < 2:
                        dbg_dump(p, dt_, f"xr_{b}_{tr}", x3row[tr % 2], b=b)
                        dbg_dump(p, dt_, f"x4pre_{b}_{tr}", x4t, b=b)
                    if not SKIP_LN:
                        ln_tile(p, x4t[:, :], x4t[:, :], sq[:, :], stats,
                                C.eps, 1024)
                    if False and th == 0 and tr < 2:
                        dbg_dump(p, dt_, f"x4post_{b}_{tr}", x4t, b=b)
                        dbg_dump(p, dt_, f"lnstats_{b}_{tr}", stats, b=b)
                    p.need("sync", p.tok(ve))
                    so = p.op("sync", "dma_start",
                              dt_["out"][b, ti * 128:(ti + 1) * 128, :],
                              x4t[:, :], inc=(p.sem(f"st_o{tr % 2}"), 16))
                    p.need("vector", so[0])
                    if TAIL_BARRIER:
                        p.barrier()
        p.need("sync", p.tok(p.sem("st_o0")), p.tok(p.sem("st_o1")))
    p.barrier()
